# version 06
# baseline (speedup 1.0000x reference)
import numpy as np
import ml_dtypes
import concourse.bass as bass
import concourse.mybir as mybir
from concourse.bass_utils import run_bass_kernel_spmd

F32 = mybir.dt.float32
BF16 = mybir.dt.bfloat16
AF = mybir.ActivationFunctionType
ALU = mybir.AluOpType
AX = mybir.AxisListType
BF = ml_dtypes.bfloat16

H, DH, DM = 32, 128, 4096
S_TOT = 16384
NCORES = 8
T = 2048
HALO = 2560
CH = 1024
NCHUNK = 2
ROPE_BASE = 10000.0
SC_HALF = float((1.0 / np.sqrt(DH)) ** 0.5)
EPS = 1e-5

ENGS = ("pe", "act", "dve", "sync", "gp")
DEBUG = False


class Prog:
    def __init__(self):
        self.ops = {e: [] for e in ENGS}
        self.cnt = {e: 0 for e in ENGS}
        self.dma_cnt = {"sync": 0, "gp": 0}
        self.last_wait = {e: {} for e in ENGS}
        self.final_waits = []

    def _avail(self, sem):
        if sem == "syd":
            return 16 * self.dma_cnt["sync"]
        if sem == "gpd":
            return 16 * self.dma_cnt["gp"]
        return self.cnt[sem]

    def emit(self, eng, fn, deps=()):
        waits = []
        for d in deps:
            if d is None:
                continue
            sem, val = d
            if sem == eng:
                continue
            assert val <= self._avail(sem), f"dep on future: {sem} {val} > {self._avail(sem)}"
            lw = self.last_wait[eng]
            if lw.get(sem, -1) >= val:
                continue
            lw[sem] = val
            waits.append((sem, val))
        if eng in ("sync", "gp"):
            self.dma_cnt[eng] += 1
            dsem = "syd" if eng == "sync" else "gpd"
            self.ops[eng].append((waits, fn, (dsem, 16)))
            return (dsem, 16 * self.dma_cnt[eng])
        self.cnt[eng] += 1
        self.ops[eng].append((waits, fn, (eng, 1)))
        return (eng, self.cnt[eng])

    def join(self):
        return [("pe", self.cnt["pe"]), ("act", self.cnt["act"]), ("dve", self.cnt["dve"]),
                ("syd", 16 * self.dma_cnt["sync"]), ("gpd", 16 * self.dma_cnt["gp"])]


def ecopy(e, out, in_):
    f = getattr(e, "tensor_copy", None)
    if f is None:
        return e.copy(out, in_)
    return f(out, in_)


def rope_cache(positions):
    inv_freq = 1.0 / ROPE_BASE ** (np.arange(0, DH, 2, dtype=np.float32) / DH)
    f = np.outer(positions.astype(np.float32), inv_freq)
    cos = np.zeros((len(positions), DH), dtype=np.float32)
    sin = np.zeros((len(positions), DH), dtype=np.float32)
    cos[:, 0::2] = np.cos(f); cos[:, 1::2] = np.cos(f)
    sin[:, 0::2] = np.sin(f); sin[:, 1::2] = np.sin(f)
    return cos, sin


def build_program():
    nc = bass.Bass("TRN2", target_bir_lowering=False)
    d_xh = nc.dram_tensor("xh", [HALO, DM], F32, kind="ExternalInput").ap()
    d_xr = nc.dram_tensor("xr", [NCHUNK, 8, 128, 8, 512], F32, kind="ExternalInput").ap()
    d_wq = nc.dram_tensor("wq", [H, 128, 32, 128], F32, kind="ExternalInput").ap()
    d_wo = nc.dram_tensor("wo", [8, 8, 128, 4, 512], F32, kind="ExternalInput").ap()
    d_wk = nc.dram_tensor("wk", [128, 32, 128], F32, kind="ExternalInput").ap()
    d_wv = nc.dram_tensor("wv", [128, 32, 128], F32, kind="ExternalInput").ap()
    d_awr = nc.dram_tensor("awr", [128, 32, 64], F32, kind="ExternalInput").ap()
    d_wr = nc.dram_tensor("wr", [128, 32, 4], F32, kind="ExternalInput").ap()
    d_bt = nc.dram_tensor("bt", [8, 64, 512], F32, kind="ExternalInput").ap()
    d_cos = nc.dram_tensor("cosT", [128, HALO], BF16, kind="ExternalInput").ap()
    d_sin = nc.dram_tensor("sinT", [128, HALO], BF16, kind="ExternalInput").ap()
    d_alf = nc.dram_tensor("alf", [128, 16], F32, kind="ExternalInput").ap()
    d_oma = nc.dram_tensor("oma", [128, 16], F32, kind="ExternalInput").ap()
    d_cor = nc.dram_tensor("cor", [128, 16], F32, kind="ExternalInput").ap()
    d_idb = nc.dram_tensor("idb", [128, 128], BF16, kind="ExternalInput").ap()
    d_idf = nc.dram_tensor("idf", [128, 128], F32, kind="ExternalInput").ap()
    d_exe = nc.dram_tensor("exe", [4, 64], BF16, kind="ExternalInput").ap()
    d_y = nc.dram_tensor("y", [T, DM], F32, kind="ExternalOutput").ap()
    if DEBUG:
        d_dbg = {
            "dkT": (nc.dram_tensor("dkT", [128, HALO], BF16, kind="ExternalOutput").ap(), None),
            "dvN": (nc.dram_tensor("dvN", [128, 20, 128], BF16, kind="ExternalOutput").ap(), None),
            "dxnT": (nc.dram_tensor("dxnT", [128, 32, CH], BF16, kind="ExternalOutput").ap(), None),
            "datT": (nc.dram_tensor("datT", [128, 32, CH], BF16, kind="ExternalOutput").ap(), None),
            "dmidp": (nc.dram_tensor("dmidp", [64, 1024], BF16, kind="ExternalOutput").ap(), None),
            "dgbf": (nc.dram_tensor("dgbf", [128, 32], BF16, kind="ExternalOutput").ap(), None),
            "dst": (nc.dram_tensor("dst", [128, 8], F32, kind="ExternalOutput").ap(), None),
            "datT0": (nc.dram_tensor("datT0", [128, 32, CH], BF16, kind="ExternalOutput").ap(), None),
            "dxnT0": (nc.dram_tensor("dxnT0", [128, 32, CH], BF16, kind="ExternalOutput").ap(), None),
            "dmidp0": (nc.dram_tensor("dmidp0", [64, 1024], BF16, kind="ExternalOutput").ap(), None),
        }

    P = Prog()

    def sballoc(name, shape, dtype):
        g = nc.sbuf_tensor(name, shape, dtype)
        return g.__enter__(), g

    def psalloc(name, shape, dtype):
        g = nc.psum_tensor(name, shape, dtype)
        return g.__enter__(), g

    xnT, _ = sballoc("xnT", [128, 32, CH], BF16)
    atT, _ = sballoc("atT", [128, 32, CH], BF16)
    kT, _ = sballoc("kT", [128, HALO], BF16)
    vN, _ = sballoc("vN", [128, 20, 128], BF16)
    cosT, _ = sballoc("cosS", [128, HALO], BF16)
    sinT, _ = sballoc("sinS", [128, HALO], BF16)
    wqs = [sballoc(f"wqs{i}", [128, 32, 128], BF16)[0] for i in range(2)]
    awrs, _ = sballoc("awrs", [128, 32, 64], BF16)
    wrs, _ = sballoc("wrs", [128, 32, 4], BF16)
    bts = [sballoc(f"bts{i}", [64, 512], BF16)[0] for i in range(2)]
    idb, _ = sballoc("idbS", [128, 128], BF16)
    idf, _ = sballoc("idfS", [128, 128], F32)
    alf, _ = sballoc("alfS", [128, 16], F32)
    oma, _ = sballoc("omaS", [128, 16], F32)
    cor, _ = sballoc("corS", [128, 16], F32)
    eps_sb, _ = sballoc("eps", [128, 1], F32)
    c_inv, _ = sballoc("cinv", [128, 1], F32)
    c_zero, _ = sballoc("czero", [128, 1], F32)
    c_qtr, _ = sballoc("cqtr", [128, 1], F32)
    dscr, _ = sballoc("dscr", [128, 1], F32)
    exe_sb, _ = sballoc("exeS", [4, 64], BF16)
    xpool = [sballoc(f"xpool{i}", [128, 2048], F32)[0] for i in range(2)]
    xsth = xpool
    rt1 = xpool[0][:, 0:1024]
    rt2 = xpool[0][:, 1024:2048]
    bpool, _ = sballoc("bpool", [128, 4096], BF16)
    xbfp = bpool
    qrb = bpool[:, 0:1024]
    PbA = bpool[:, 1024:2560]
    PbB = bpool[:, 2560:4096]
    gts = bpool[0:4, 0:1024]
    wor = bpool
    gbig = bpool[0:64, 1024:2048]
    kpool, _ = sballoc("kpool", [128, 1024], F32)
    kt1 = kpool[:, 0:512]
    kt2 = kpool[:, 512:1024]
    lg4 = kpool[0:4, 0:1024]
    outb = [kpool[:, 0:512], kpool[:, 512:1024]]
    PTs, _ = sballoc("PTs", [128, 6, 256], BF16)
    midp, _ = sballoc("midp", [64, 1024], BF16)
    gl, _ = sballoc("gl", [128, 32], F32)
    gbf, _ = sballoc("gbf", [128, 32], BF16)
    ge4 = [sballoc(f"ge4{i}", [128, 4], F32)[0] for i in range(2)]
    mk4 = [sballoc(f"mk4{i}", [128, 4], F32)[0] for i in range(2)]
    g4 = [sballoc(f"g4{i}", [128, 4], F32)[0] for i in range(2)]
    stat = {}
    for nm in ("sqh0", "sqh1", "smh0", "smh1", "sm", "mean", "sq", "msq", "var", "sd", "rs", "mrs"):
        stat[nm] = [sballoc(f"{nm}{i}", [128, 1], F32)[0] for i in range(2)]
    asc = {}
    for nm in ("sA0", "sB0", "sC0", "sA1", "sB1", "sC1", "r10", "r20", "r11", "r21",
               "q10", "q20", "q11", "q21", "u0", "v0", "w0", "u1", "v1", "w1"):
        asc[nm] = sballoc("a" + nm, [128, 1], F32)[0]
    gsc = {}
    for nm in ("p1", "p2", "p3", "p4", "t1", "t2", "thr", "m1", "nm1", "z1", "z2", "zz", "wg"):
        gsc[nm] = [sballoc(f"g{nm}{i}", [128, 1], F32)[0] for i in range(2)]

    tcons = {}
    for key, dst, src in (("cos", cosT, d_cos), ("sin", sinT, d_sin), ("alf", alf, d_alf),
                          ("oma", oma, d_oma), ("cor", cor, d_cor), ("idb", idb, d_idb),
                          ("idf", idf, d_idf), ("exe", exe_sb, d_exe)):
        tcons[key] = P.emit("sync", lambda e, a=dst, b=src: e.dma_start(out=a[:], in_=b))
    tok_wk = P.emit("gp", lambda e: e.dma_start(out=wqs[0][:], in_=d_wk))
    tok_wv = P.emit("gp", lambda e: e.dma_start(out=wqs[1][:], in_=d_wv))
    tok_awr = P.emit("gp", lambda e: e.dma_start(out=awrs[:], in_=d_awr))
    tok_wr = P.emit("gp", lambda e: e.dma_start(out=wrs[:], in_=d_wr))
    tok_eps = P.emit("dve", lambda e: e.memset(eps_sb[:], EPS))
    tok_eps = P.emit("dve", lambda e: e.memset(c_inv[:], 1.0 / DM), deps=[tok_eps])
    tok_eps = P.emit("dve", lambda e: e.memset(c_zero[:], 0.0), deps=[tok_eps])
    tok_eps = P.emit("dve", lambda e: e.memset(c_qtr[:], 0.25), deps=[tok_eps])

    def ln_tile(r0, tcol, par, lnt, prev_par, xbf_war, prev_ev, extra_load_deps):
        s = stat
        t_ld = []
        for j in range(2):
            t_ld.append(P.emit("sync", lambda e, j=j: e.dma_start(
                out=xsth[j][:], in_=d_xh[r0:r0 + 128, 2048 * j:2048 * j + 2048]),
                deps=list(prev_par.get(j, [])) + list(extra_load_deps)))
        t_sq, t_sm = [], []
        for j in range(2):
            t_sq.append(P.emit("act", lambda e, j=j: e.activation(
                xbfp[:, 2048 * j:2048 * j + 2048], xsth[j][:], AF.Square,
                accum_out=s["sqh%d" % j][par][:]), deps=[t_ld[j]] + list(xbf_war)))
            t_sm.append(P.emit("act", lambda e, j=j: e.activation(
                xbfp[:, 2048 * j:2048 * j + 2048], xsth[j][:], AF.Copy,
                accum_out=s["smh%d" % j][par][:]), deps=[t_ld[j]] + list(xbf_war)))
        t1 = P.emit("dve", lambda e: e.tensor_add(s["sm"][par][:], s["smh0"][par][:], s["smh1"][par][:]), deps=[t_sm[0], t_sm[1]])
        t3 = P.emit("dve", lambda e: e.tensor_add(s["sq"][par][:], s["sqh0"][par][:], s["sqh1"][par][:]), deps=[t_sq[0], t_sq[1]])
        t2 = P.emit("dve", lambda e: e.tensor_mul(s["mean"][par][:], s["sm"][par][:], c_inv[:]), deps=[t1, tok_eps])
        t5 = P.emit("dve", lambda e: e.tensor_mul(s["var"][par][:], s["sq"][par][:], c_inv[:]), deps=[t3, tok_eps])
        t4 = P.emit("dve", lambda e: e.tensor_mul(s["msq"][par][:], s["mean"][par][:], s["mean"][par][:]), deps=[t2])
        P.emit("dve", lambda e: e.tensor_copy(dscr[:], c_zero[:]), deps=[tok_eps])
        t6 = P.emit("dve", lambda e: e.tensor_sub(s["var"][par][:], s["var"][par][:], s["msq"][par][:]), deps=[t4, t5])
        t7 = P.emit("act", lambda e: e.activation(s["sd"][par][:], s["var"][par][:], AF.Sqrt, bias=eps_sb[:], scale=1.0), deps=[t6, tok_eps])
        t7b = P.emit("dve", lambda e: e.reciprocal(s["rs"][par][:], s["sd"][par][:]), deps=[t7])
        P.emit("dve", lambda e: e.tensor_copy(dscr[:], c_zero[:]), deps=[tok_eps])
        t_nm = []
        for j in range(2):
            t_nm.append(P.emit("dve", lambda e, j=j: e.tensor_scalar(
                out=xbfp[:, 2048 * j:2048 * j + 2048], in0=xsth[j][:],
                scalar1=s["mean"][par][:], scalar2=s["rs"][par][:],
                op0=ALU.subtract, op1=ALU.mult), deps=[t7b, t2] + list(xbf_war)))
        ev_toks = []
        for bi in range(8):
            lp = bi % 2
            tps = None
            dd0 = [t_nm[0] if 4 * bi < 16 else t_nm[1], t_nm[1] if 4 * bi + 3 >= 16 else None,
                   tcons["idb"], prev_ev.get(lp)]
            for k in range(4):
                c = 4 * bi + k
                tps = P.emit("pe", lambda e, c=c, k=k, lp=lp: e.matmul(
                    lnt[lp][:, k, :], xbfp[:, 128 * c:128 * c + 128], idb[:],
                    is_transpose=True, start=(k == 0), stop=(k == 3)),
                    deps=dd0 if k == 0 else [])
            eng = "act" if bi % 2 == 0 else "dve"
            ev = P.emit(eng, lambda e, bi=bi, lp=lp: ecopy(e,
                xnT[:, 4 * bi:4 * bi + 4, tcol:tcol + 128], lnt[lp][:]), deps=[tps])
            prev_ev[lp] = ev
            ev_toks.append(ev)
        prev_par[0] = [t_nm[0], t_sm[0], t_sq[0]]
        prev_par[1] = [t_nm[1], t_sm[1], t_sq[1]]
        return t_nm, ev_toks, tps

    lnt = [psalloc(f"lnp{i}", [128, 4, 128], BF16) for i in range(2)]
    kp, kp_g = psalloc("kp", [128, 512], F32)
    vp, vp_g = psalloc("vp", [128, 128], F32)

    prev_par = {}
    prev_ev = {0: None, 1: None}
    prev_k_rope = None
    prev_v_ev = None
    last_tp = None
    for kc in range(5):
        evs = []
        for t2 in range(4):
            t_nm, ev_toks, last_tp = ln_tile(512 * kc + 128 * t2, 128 * t2, t2 % 2,
                                             [lnt[0][0], lnt[1][0]], prev_par,
                                             [last_tp] if last_tp else [], prev_ev, [])
            evs.extend(ev_toks)
        dk = evs + [tok_wk]
        if prev_k_rope is not None:
            dk.append(prev_k_rope)
        tkm = None
        for c in range(32):
            tkm = P.emit("pe", lambda e, c=c: e.matmul(
                kp[:], wqs[0][:, c, :], xnT[:, c, 0:512], start=(c == 0), stop=(c == 31)),
                deps=dk if c == 0 else [])
        co = 512 * kc
        tr1 = P.emit("dve", lambda e, co=co: e.tensor_mul(kt1[:], kp[:], cosT[:, co:co + 512]), deps=[tkm, tcons["cos"]])
        tr2 = P.emit("dve", lambda e, co=co: e.tensor_mul(kt2[0:64, :], kp[64:128, :], sinT[0:64, co:co + 512]), deps=[tkm, tcons["sin"]])
        tr3 = P.emit("dve", lambda e, co=co: e.tensor_mul(kt2[64:128, :], kp[0:64, :], sinT[64:128, co:co + 512]), deps=[tkm])
        prev_k_rope = P.emit("dve", lambda e, co=co: e.tensor_add(kT[:, co:co + 512], kt1[:], kt2[:]), deps=[tr1, tr2, tr3])
        for t2 in range(4):
            dv = [evs[-1], tok_wv]
            if prev_v_ev is not None:
                dv.append(prev_v_ev)
            tvm = None
            for c in range(32):
                tvm = P.emit("pe", lambda e, c=c, t2=t2: e.matmul(
                    vp[:], xnT[:, c, 128 * t2:128 * t2 + 128], wqs[1][:, c, :],
                    start=(c == 0), stop=(c == 31)), deps=dv if c == 0 else [])
            prev_v_ev = P.emit("act", lambda e, kc=kc, t2=t2: ecopy(e,
                vN[:, 4 * kc + t2, :], vp[:]), deps=[tvm])

    vp_g.__exit__(None, None, None)
    kp_g.__exit__(None, None, None)
    lnt[1][1].__exit__(None, None, None)
    lnt[0][1].__exit__(None, None, None)

    last_join = P.join()
    tok_y_final = []

    for ch in range(NCHUNK):
        lnt = [psalloc(f"ln{ch}{i}", [128, 4, 128], BF16) for i in range(2)]
        ln_join = last_join
        tok_wq = [None, None]
        tok_wq[0] = P.emit("gp", lambda e: e.dma_start(out=wqs[0][:], in_=d_wq[0]), deps=ln_join)
        tok_wq[1] = P.emit("gp", lambda e: e.dma_start(out=wqs[1][:], in_=d_wq[1]), deps=ln_join)
        prev_par = dict(prev_par)
        first = True
        for t in range(8):
            extra = list(ln_join) if first else []
            t_nm, ev_toks, last_tp = ln_tile(256 + CH * ch + 128 * t, 128 * t, t % 2,
                                             [lnt[0][0], lnt[1][0]], prev_par,
                                             ([last_tp] if last_tp else []) + (list(ln_join) if first else []),
                                             prev_ev, extra)
            first = False
        lnt[1][1].__exit__(None, None, None)
        lnt[0][1].__exit__(None, None, None)
        xnT_ready = P.join()

        qp, qp_g = psalloc(f"qps{ch}", [128, 1024], F32)
        smp, smp_g = psalloc(f"smps{ch}", [128, 1024], F32)
        stp, stp_g = psalloc(f"stps{ch}", [128, 512], F32)
        pt0, pt0_g = psalloc(f"ptps0{ch}", [128, 6, 128], BF16)
        pt1, pt1_g = psalloc(f"ptps1{ch}", [128, 6, 128], BF16)
        avp, avp_g = psalloc(f"avps{ch}", [128, 256], F32)

        qoff = 256 + CH * ch
        st_attn = {
            "sig_prev": None,
            "rm_prev": {0: None, 1: None},
            "sc_prev": {0: None, 1: None},
            "pt_prev": {0: None, 1: None},
            "ptev_prev": {0: None, 1: None},
            "av_prev": None,
            "avev_prev": None,
            "rope": None,
        }
        war_atT = list(ln_join)

        def emit_q(h, extra):
            toks = []
            for nh in range(2):
                dd = [tok_wq[h % 2]] + xnT_ready + list(extra) + [st_attn["rope"]]
                tq = None
                for c in range(32):
                    tq = P.emit("pe", lambda e, c=c, nh=nh, h=h: e.matmul(
                        qp[:, 512 * nh:512 * nh + 512], wqs[h % 2][:, c, :],
                        xnT[:, c, 512 * nh:512 * nh + 512],
                        start=(c == 0), stop=(c == 31)), deps=dd if c == 0 else [])
                o = qoff + 512 * nh
                q0 = 512 * nh
                ta = P.emit("dve", lambda e, o=o, q0=q0: e.tensor_mul(
                    rt1[:, q0:q0 + 512], qp[:, q0:q0 + 512], cosT[:, o:o + 512]), deps=[tq, tcons["cos"]])
                tb = P.emit("dve", lambda e, o=o, q0=q0: e.tensor_mul(
                    rt2[0:64, q0:q0 + 512], qp[64:128, q0:q0 + 512], sinT[0:64, o:o + 512]), deps=[tq, tcons["sin"]])
                tc = P.emit("dve", lambda e, o=o, q0=q0: e.tensor_mul(
                    rt2[64:128, q0:q0 + 512], qp[0:64, q0:q0 + 512], sinT[64:128, o:o + 512]), deps=[tq])
                td = P.emit("dve", lambda e, q0=q0: e.tensor_add(
                    qrb[:, q0:q0 + 512], rt1[:, q0:q0 + 512], rt2[:, q0:q0 + 512]), deps=[ta, tb, tc] + list(extra))
                toks.append(td)
            st_attn["rope"] = toks[1]

        def seg_S(s4):
            sg = 4 * ch + s4
            klo = 256 * sg
            sp = st_attn["sig_prev"]
            res = {}
            for qt in range(2):
                dd = [st_attn["rope"], prev_k_rope]
                if sp is not None:
                    dd.append(sp[qt])
                res[qt] = P.emit("pe", lambda e, qt=qt, klo=klo, s4=s4: e.matmul(
                    smp[:, 512 * qt:512 * qt + 512],
                    qrb[:, 256 * s4 + 128 * qt:256 * s4 + 128 * qt + 128],
                    kT[:, klo:klo + 512], start=True, stop=True), deps=dd)
            ddt = [st_attn["rope"], prev_k_rope]
            if sp is not None:
                ddt.append(sp[2])
            tt0 = P.emit("pe", lambda e, klo=klo, s4=s4: e.matmul(
                stp[:, 0:256], qrb[:, 256 * s4:256 * s4 + 128],
                kT[:, klo + 512:klo + 768], start=True, stop=False), deps=ddt)
            res["t"] = P.emit("pe", lambda e, klo=klo, s4=s4: e.matmul(
                stp[:, 256:512], qrb[:, 256 * s4 + 128:256 * s4 + 256],
                kT[:, klo + 512:klo + 768], start=False, stop=True), deps=[tt0])
            return res

        def seg_post(s4, S):
            sg = 4 * ch + s4
            col = 2 * sg
            sig = {}
            sc_out = []
            for qt in range(2):
                scp = st_attn["sc_prev"][qt]
                rmp = st_attn["rm_prev"][qt]
                war = list(scp) if scp else []
                warr = list(rmp) if rmp else []
                sig[f"A{qt}"] = P.emit("act", lambda e, qt=qt: e.activation(
                    PbA[:, 768 * qt:768 * qt + 256], smp[:, 512 * qt:512 * qt + 256], AF.Sigmoid,
                    accum_out=asc[f"sA{qt}"][:]), deps=[S[qt]] + war[0:1] + warr)
                sig[f"B{qt}"] = P.emit("act", lambda e, qt=qt: e.activation(
                    PbA[:, 768 * qt + 256:768 * qt + 512], smp[:, 512 * qt + 256:512 * qt + 512], AF.Sigmoid,
                    accum_out=asc[f"sB{qt}"][:]), deps=[S[qt]] + war[1:2])
                sig[f"C{qt}"] = P.emit("act", lambda e, qt=qt: e.activation(
                    PbA[:, 768 * qt + 512:768 * qt + 768], stp[:, 256 * qt:256 * qt + 256], AF.Sigmoid,
                    accum_out=asc[f"sC{qt}"][:]), deps=[S["t"]] + war[2:3])
            cc = {}
            for step in range(8):
                for qt in range(2):
                    a = asc
                    r1, r2 = a[f"r1{qt}"], a[f"r2{qt}"]
                    q1, q2 = a[f"q1{qt}"], a[f"q2{qt}"]
                    u, v, w = a[f"u{qt}"], a[f"v{qt}"], a[f"w{qt}"]
                    if step == 0:
                        cc[(1, qt)] = P.emit("dve", lambda e, qt=qt, r1=r1: e.tensor_add(r1[:], asc[f"sA{qt}"][:], asc[f"sB{qt}"][:]), deps=[sig[f"A{qt}"], sig[f"B{qt}"]])
                    elif step == 1:
                        cc[(2, qt)] = P.emit("dve", lambda e, qt=qt, r2=r2: e.tensor_add(r2[:], asc[f"sB{qt}"][:], asc[f"sC{qt}"][:]), deps=[sig[f"B{qt}"], sig[f"C{qt}"]])
                    elif step == 2:
                        cc[(3, qt)] = P.emit("dve", lambda e, r2=r2, col=col, qt=qt: e.tensor_sub(r2[:], r2[:], cor[:, col + qt:col + qt + 1]), deps=[cc[(2, qt)], tcons["cor"]])
                    elif step == 3:
                        cc[(4, qt)] = P.emit("dve", lambda e, r1=r1, q1=q1: e.reciprocal(q1[:], r1[:]), deps=[cc[(1, qt)]])
                    elif step == 4:
                        cc[(5, qt)] = P.emit("dve", lambda e, r2=r2, q2=q2: e.reciprocal(q2[:], r2[:]), deps=[cc[(3, qt)]])
                    elif step == 5:
                        cc[(6, qt)] = P.emit("dve", lambda e, q1=q1, u=u, col=col, qt=qt: e.tensor_mul(u[:], oma[:, col + qt:col + qt + 1], q1[:]), deps=[cc[(4, qt)], tcons["oma"]])
                    elif step == 6:
                        cc[(7, qt)] = P.emit("dve", lambda e, q2=q2, v=v, col=col, qt=qt: e.tensor_mul(v[:], alf[:, col + qt:col + qt + 1], q2[:]), deps=[cc[(5, qt)], tcons["alf"]])
                    elif step == 7:
                        cc[(8, qt)] = P.emit("dve", lambda e, u=u, v=v, w=w: e.tensor_add(w[:], u[:], v[:]), deps=[cc[(6, qt)], cc[(7, qt)]])
            for qt in range(2):
                a = asc
                u, v, w = a[f"u{qt}"], a[f"v{qt}"], a[f"w{qt}"]
                c6, c7, c8 = cc[(6, qt)], cc[(7, qt)], cc[(8, qt)]
                ptw = st_attn["pt_prev"][qt]
                if qt == 0:
                    s1 = P.emit("act", lambda e, qt=qt, u=u: e.activation(
                        PbB[:, 768 * qt:768 * qt + 256], PbA[:, 768 * qt:768 * qt + 256],
                        AF.Copy, scale=u[:]), deps=[c6, ptw])
                    s2 = P.emit("act", lambda e, qt=qt, w=w: e.activation(
                        PbB[:, 768 * qt + 256:768 * qt + 512], PbA[:, 768 * qt + 256:768 * qt + 512],
                        AF.Copy, scale=w[:]), deps=[c8])
                    s3 = P.emit("act", lambda e, qt=qt, v=v: e.activation(
                        PbB[:, 768 * qt + 512:768 * qt + 768], PbA[:, 768 * qt + 512:768 * qt + 768],
                        AF.Copy, scale=v[:]), deps=[c7])
                else:
                    s1 = P.emit("dve", lambda e, qt=qt, u=u: e.tensor_scalar(
                        out=PbB[:, 768 * qt:768 * qt + 256], in0=PbA[:, 768 * qt:768 * qt + 256],
                        scalar1=u[:], scalar2=None, op0=ALU.mult), deps=[c6, ptw])
                    s2 = P.emit("dve", lambda e, qt=qt, w=w: e.tensor_scalar(
                        out=PbB[:, 768 * qt + 256:768 * qt + 512], in0=PbA[:, 768 * qt + 256:768 * qt + 512],
                        scalar1=w[:], scalar2=None, op0=ALU.mult), deps=[c8])
                    s3 = P.emit("dve", lambda e, qt=qt, v=v: e.tensor_scalar(
                        out=PbB[:, 768 * qt + 512:768 * qt + 768], in0=PbA[:, 768 * qt + 512:768 * qt + 768],
                        scalar1=v[:], scalar2=None, op0=ALU.mult), deps=[c7])
                st_attn["sc_prev"][qt] = (s1, s2, s3)
                st_attn["rm_prev"][qt] = [cc[(1, qt)], cc[(2, qt)]]
                sc_out.append((s1, s2, s3))
            st_attn["sig_prev"] = (
                max(sig["A0"], sig["B0"], key=lambda t: t[1]),
                max(sig["A1"], sig["B1"], key=lambda t: t[1]),
                max(sig["C0"], sig["C1"], key=lambda t: t[1]),
            )
            return sc_out

        def seg_ptav(h, s4, sc):
            sg = 4 * ch + s4
            ptev = []
            for qt in range(2):
                pt = pt0 if qt == 0 else pt1
                dd0 = list(sc[qt]) + [tcons["idb"], st_attn["ptev_prev"][qt]]
                tps = None
                for j in range(6):
                    tps = P.emit("pe", lambda e, qt=qt, j=j, pt=pt: e.matmul(
                        pt[:, j, :], PbB[:, 768 * qt + 128 * j:768 * qt + 128 * j + 128], idb[:],
                        is_transpose=True, start=(j == 0), stop=(j == 5)),
                        deps=dd0 if j == 0 else [])
                st_attn["pt_prev"][qt] = tps
                eng = "dve" if qt == 0 else "act"
                ev = P.emit(eng, lambda e, qt=qt, pt=pt: ecopy(e,
                    PTs[:, :, 128 * qt:128 * qt + 128], pt[:]),
                    deps=[tps, st_attn["av_prev"]])
                st_attn["ptev_prev"][qt] = ev
                ptev.append(ev)
            ddav = ptev + [st_attn["avev_prev"]]
            tav = None
            for j in range(6):
                tav = P.emit("pe", lambda e, j=j, sg=sg: e.matmul(
                    avp[:], vN[:, 2 * sg + j, :], PTs[:, j, :], start=(j == 0), stop=(j == 5)),
                    deps=ddav if j == 0 else [])
            st_attn["av_prev"] = tav
            aev = P.emit("dve", lambda e, h=h, s4=s4: e.tensor_copy(
                atT[:, h, 256 * s4:256 * s4 + 256], avp[:]), deps=[tav] + war_atT)
            st_attn["avev_prev"] = aev
            return aev

        emit_q(0, xnT_ready)
        last_aev = None
        for h in range(H):
            S0 = seg_S(0); sc0 = seg_post(0, S0)
            S1 = seg_S(1); seg_ptav(h, 0, sc0); sc1 = seg_post(1, S1)
            S2 = seg_S(2); seg_ptav(h, 1, sc1); sc2 = seg_post(2, S2)
            S3 = seg_S(3); seg_ptav(h, 2, sc2); sc3 = seg_post(3, S3)
            if h + 1 < H:
                tok_wq[(h + 1) % 2] = P.emit("gp", lambda e, h=h: e.dma_start(
                    out=wqs[(h + 1) % 2][:], in_=d_wq[h + 1]), deps=[S3["t"]])
                emit_q(h + 1, [S3["t"]])
            last_aev = seg_ptav(h, 3, sc3)

        attn_join = P.join()
        for g in (avp_g, pt1_g, pt0_g, stp_g, smp_g, qp_g):
            g.__exit__(None, None, None)

        mid, mid_g = psalloc(f"midps{ch}", [64, 1024], F32)
        gtp, gtp_g = psalloc(f"gtps{ch}", [4, 1024], BF16)
        lgp, lgp_g = psalloc(f"lgps{ch}", [4, 1024], F32)
        t4p, t4p_g = psalloc(f"t4ps{ch}", [128, 32], F32)

        t_mid = []
        for nh in range(2):
            tm = None
            for c in range(32):
                tm = P.emit("pe", lambda e, c=c, nh=nh: e.matmul(
                    mid[:, 512 * nh:512 * nh + 512], awrs[:, c, :],
                    xnT[:, c, 512 * nh:512 * nh + 512], start=(c == 0), stop=(c == 31)),
                    deps=(attn_join + [tok_awr]) if c == 0 else [])
            t_mid.append(tm)
        t_lg = []
        for nh in range(2):
            tm = None
            for c in range(32):
                tm = P.emit("pe", lambda e, c=c, nh=nh: e.matmul(
                    lgp[:, 512 * nh:512 * nh + 512], wrs[:, c, :],
                    xnT[:, c, 512 * nh:512 * nh + 512], start=(c == 0), stop=(c == 31)),
                    deps=[tok_wr] if c == 0 else [])
            t_lg.append(tm)
        t_lgev = P.emit("act", lambda e: e.copy(lg4[:], lgp[:]), deps=t_lg)
        tt4 = None
        for t in range(8):
            tt4 = P.emit("pe", lambda e, t=t: e.matmul(
                t4p[:, 4 * t:4 * t + 4], lg4[0:4, 128 * t:128 * t + 128], idf[0:4, 0:4],
                is_transpose=True, start=(t == 0), stop=(t == 7)), deps=[t_lgev, tcons["idf"]])
        t_glev = P.emit("act", lambda e: e.copy(gl[:], t4p[:]), deps=[tt4])
        g_toks = [None] * 8
        bb = {}
        for pair in range(4):
            for step in range(18):
                for q in range(2):
                    t = 2 * pair + q
                    G = lambda nm: gsc[nm][q]
                    d_prev = [g_toks[t - 2]] if t >= 2 else []
                    if step == 0:
                        bb[(1, q)] = P.emit("dve", lambda e, t=t, q=q: e.tensor_max(gsc["p1"][q][:], gl[:, 4 * t:4 * t + 1], gl[:, 4 * t + 1:4 * t + 2]), deps=[t_glev] + d_prev)
                    elif step == 1:
                        bb[(2, q)] = P.emit("dve", lambda e, t=t, q=q: e.tensor_tensor(gsc["p2"][q][:], gl[:, 4 * t:4 * t + 1], gl[:, 4 * t + 1:4 * t + 2], op=ALU.min), deps=[])
                    elif step == 2:
                        bb[(3, q)] = P.emit("dve", lambda e, t=t, q=q: e.tensor_max(gsc["p3"][q][:], gl[:, 4 * t + 2:4 * t + 3], gl[:, 4 * t + 3:4 * t + 4]), deps=[])
                    elif step == 3:
                        bb[(4, q)] = P.emit("dve", lambda e, t=t, q=q: e.tensor_tensor(gsc["p4"][q][:], gl[:, 4 * t + 2:4 * t + 3], gl[:, 4 * t + 3:4 * t + 4], op=ALU.min), deps=[])
                    elif step == 4:
                        bb[(5, q)] = P.emit("dve", lambda e, q=q: e.tensor_tensor(gsc["t1"][q][:], gsc["p1"][q][:], gsc["p3"][q][:], op=ALU.min), deps=[bb[(1, q)], bb[(3, q)]])
                    elif step == 5:
                        bb[(6, q)] = P.emit("dve", lambda e, q=q: e.tensor_max(gsc["t2"][q][:], gsc["p2"][q][:], gsc["p4"][q][:]), deps=[bb[(2, q)], bb[(4, q)]])
                    elif step == 6:
                        bb[(8, q)] = P.emit("dve", lambda e, q=q: e.tensor_max(gsc["m1"][q][:], gsc["p1"][q][:], gsc["p3"][q][:]), deps=[bb[(1, q)], bb[(3, q)]])
                    elif step == 7:
                        bb[(7, q)] = P.emit("dve", lambda e, q=q: e.tensor_max(gsc["thr"][q][:], gsc["t1"][q][:], gsc["t2"][q][:]), deps=[bb[(5, q)], bb[(6, q)]])
                    elif step == 8:
                        bb[(9, q)] = P.emit("dve", lambda e, q=q: e.tensor_sub(gsc["nm1"][q][:], c_zero[:], gsc["m1"][q][:]), deps=[bb[(8, q)], tok_eps])
                    elif step == 9:
                        bb[(11, q)] = P.emit("dve", lambda e, t=t, q=q: e.tensor_scalar(out=mk4[q][:], in0=gl[:, 4 * t:4 * t + 4], scalar1=gsc["thr"][q][:], scalar2=None, op0=ALU.is_ge), deps=[bb[(7, q)]])
                    elif step == 10:
                        bb[(10, q)] = P.emit("act", lambda e, t=t, q=q: e.activation(ge4[q][:], gl[:, 4 * t:4 * t + 4], AF.Exp, bias=gsc["nm1"][q][:], scale=1.0), deps=[bb[(9, q)]])
                    elif step == 11:
                        bb[(12, q)] = P.emit("dve", lambda e, q=q: e.tensor_mul(g4[q][:], ge4[q][:], mk4[q][:]), deps=[bb[(10, q)], bb[(11, q)]])
                    elif step == 12:
                        bb[(13, q)] = P.emit("dve", lambda e, q=q: e.tensor_add(gsc["z1"][q][:], g4[q][:, 0:1], g4[q][:, 1:2]), deps=[bb[(12, q)]])
                    elif step == 13:
                        bb[(14, q)] = P.emit("dve", lambda e, q=q: e.tensor_add(gsc["z2"][q][:], g4[q][:, 2:3], g4[q][:, 3:4]), deps=[bb[(12, q)]])
                    elif step == 14:
                        bb[(15, q)] = P.emit("dve", lambda e, q=q: e.tensor_add(gsc["zz"][q][:], gsc["z1"][q][:], gsc["z2"][q][:]), deps=[bb[(13, q)], bb[(14, q)]])
                    elif step == 15:
                        bb[(16, q)] = P.emit("dve", lambda e, q=q: e.tensor_mul(gsc["t2"][q][:], gsc["zz"][q][:], c_qtr[:]), deps=[bb[(15, q)], tok_eps])
                    elif step == 16:
                        bb[(17, q)] = P.emit("dve", lambda e, q=q: e.reciprocal(gsc["wg"][q][:], gsc["t2"][q][:]), deps=[bb[(16, q)]])
                    elif step == 17:
                        g_toks[t] = P.emit("dve", lambda e, t=t, q=q: e.tensor_scalar(out=gbf[:, 4 * t:4 * t + 4], in0=g4[q][:], scalar1=gsc["wg"][q][:], scalar2=None, op0=ALU.mult), deps=[bb[(17, q)], bb[(12, q)]])
        tgt = None
        for t in range(8):
            tgt = P.emit("pe", lambda e, t=t: e.matmul(
                gtp[:, 128 * t:128 * t + 128], gbf[:, 4 * t:4 * t + 4], idb[:],
                is_transpose=True, start=(t == 0), stop=(t == 7)),
                deps=[g_toks[t], tcons["idb"]])
        t_gts = P.emit("act", lambda e: e.copy(gts[:], gtp[:]), deps=[tgt])
        t4p_g.__exit__(None, None, None)
        lgp_g.__exit__(None, None, None)
        gexp, gexp_g = psalloc(f"gxps{ch}", [64, 1024], F32)
        t_ge = []
        for nh in range(2):
            t_ge.append(P.emit("pe", lambda e, nh=nh: e.matmul(
                gexp[:, 512 * nh:512 * nh + 512], exe_sb[:], gts[0:4, 512 * nh:512 * nh + 512],
                start=True, stop=True), deps=[t_gts, tcons["exe"]]))
        t_gev = P.emit("act", lambda e: e.copy(gbig, gexp[:]), deps=t_ge)
        t_midp = P.emit("dve", lambda e: e.tensor_mul(midp[:], mid[:], gbig), deps=[t_gev] + t_mid)
        ffn_join = P.join()
        for g in (gexp_g, gtp_g, mid_g):
            g.__exit__(None, None, None)

        if DEBUG and ch == 0:
            P.emit("gp", lambda e: e.dma_start(out=d_dbg["datT0"][0], in_=atT[:]), deps=ffn_join)
            P.emit("gp", lambda e: e.dma_start(out=d_dbg["dxnT0"][0], in_=xnT[:]), deps=ffn_join)
            P.emit("gp", lambda e: e.dma_start(out=d_dbg["dmidp0"][0], in_=midp[:]), deps=ffn_join)
        otp, otg = [], []
        for t in range(8):
            h_, g_ = psalloc(f"ops{ch}{t}", [128, 512], F32)
            otp.append(h_)
            otg.append(g_)

        drain_done = {}
        out_dma = {}
        wo_units = {}
        unit_consumed = {}
        bt_tok = {}
        bt_used = {}
        xr_tok = {}
        for s in range(8):
            xr_tok[s] = []
            for hf in range(2):
                dxr = list(ffn_join) if s < 1 else [drain_done[(s - 1, t)] for t in range(4 * hf, 4 * hf + 4)]
                xr_tok[s].append(P.emit("sync", lambda e, s=s, hf=hf, ch=ch: e.dma_start(
                    out=xpool[hf][:], in_=d_xr[ch, s, :, 4 * hf:4 * hf + 4, :]), deps=dxr))
            dbt = list(ffn_join) if s < 2 else [bt_used[s - 2]]
            bt_tok[s] = P.emit("gp", lambda e, s=s: e.dma_start(
                out=bts[s % 2][:], in_=d_bt[s]), deps=dbt)
            for u in range(2):
                k = s * 8 + u
                du = list(ffn_join) if k < 2 else [unit_consumed[divmod(k - 2, 8)]]
                wo_units[(s, u)] = P.emit("gp", lambda e, s=s, u=u: e.dma_start(
                    out=wor[:, 2048 * (u % 2):2048 * (u % 2) + 2048], in_=d_wo[s, u]), deps=du)
            tlast = {}
            for d in range(32):
                u = d // 4
                for t in range(8):
                    dd = []
                    if d == 0:
                        dd = [wo_units[(s, u)], last_aev] + attn_join + list(ffn_join)
                        if s > 0:
                            dd.append(drain_done[(s - 1, t)])
                    elif d % 4 == 0 and t == 0:
                        dd = [wo_units[(s, u)]]
                    tlast[t] = P.emit("pe", lambda e, d=d, t=t, s=s, ot=otp[t]: e.matmul(
                        ot[:], atT[:, d, 128 * t:128 * t + 128],
                        wor[:, 512 * (4 * ((d // 4) % 2) + d % 4):512 * (4 * ((d // 4) % 2) + d % 4) + 512],
                        start=(d == 0), stop=False),
                        deps=dd)
                if d % 4 == 3:
                    unit_consumed[(s, u)] = tlast[7]
                    if u + 2 <= 7:
                        wo_units[(s, u + 2)] = P.emit("gp", lambda e, s=s, u=u: e.dma_start(
                            out=wor[:, 2048 * (u % 2):2048 * (u % 2) + 2048], in_=d_wo[s, u + 2]),
                            deps=[unit_consumed[(s, u)]])
            for t in range(8):
                tlast[t] = P.emit("pe", lambda e, t=t, s=s, ot=otp[t]: e.matmul(
                    ot[:], midp[:, 128 * t:128 * t + 128], bts[s % 2][:],
                    start=False, stop=True), deps=[bt_tok[s], t_midp, tlast[t]])
            bt_used[s] = tlast[7]
            for t in range(8):
                dd = [tlast[t], xr_tok[s][t // 4]]
                kprev = s * 8 + t - 2
                if kprev >= 0:
                    dd.append(out_dma[divmod(kprev, 8)])
                drain_done[(s, t)] = P.emit("dve", lambda e, s=s, t=t, ot=otp[t]: e.tensor_add(
                    outb[t % 2][:], ot[:], xpool[t // 4][:, 512 * (t % 4):512 * (t % 4) + 512]), deps=dd)
                r0 = CH * ch + 128 * t
                tok = P.emit("sync", lambda e, s=s, t=t, r0=r0: e.dma_start(
                    out=d_y[r0:r0 + 128, 512 * s:512 * s + 512], in_=outb[t % 2][:]),
                    deps=[drain_done[(s, t)]])
                out_dma[(s, t)] = tok
                tok_y_final.append(tok)

        last_join = P.join()
        for g in reversed(otg):
            g.__exit__(None, None, None)

    if DEBUG:
        stdump, _ = sballoc("stdump", [128, 8], F32)
        for ii, (nm2, pp) in enumerate((("mean", 0), ("mean", 1), ("rs", 0), ("rs", 1),
                                        ("var", 0), ("var", 1), ("sm", 0), ("sm", 1))):
            P.emit("dve", lambda e, ii=ii, nm2=nm2, pp=pp: e.tensor_copy(
                stdump[:, ii:ii + 1], stat[nm2][pp][:]), deps=last_join)
        P.emit("gp", lambda e: e.dma_start(out=d_dbg["dst"][0], in_=stdump[:]), deps=P.join())
        for nm, src in (("dkT", kT), ("dvN", vN), ("dxnT", xnT), ("datT", atT),
                        ("dmidp", midp), ("dgbf", gbf)):
            P.emit("gp", lambda e, nm=nm, src=src: e.dma_start(out=d_dbg[nm][0], in_=src[:]),
                   deps=last_join)
        P.final_waits.append(("act", "gpd", 16 * P.dma_cnt["gp"]))
    P.final_waits.append(("gp", "syd", 16 * P.dma_cnt["sync"]))
    P.final_waits.append(("act", "syd", 16 * P.dma_cnt["sync"]))

    with (
        nc.semaphore("s_pe") as s_pe,
        nc.semaphore("s_act") as s_act,
        nc.semaphore("s_dve") as s_dve,
        nc.semaphore("s_syd") as s_syd,
        nc.semaphore("s_gpd") as s_gpd,
        nc.Block() as block,
    ):
        sems = {"pe": s_pe, "act": s_act, "dve": s_dve, "syd": s_syd, "gpd": s_gpd}

        def replay(eng_obj, name):
            for waits, fn, inc in P.ops[name]:
                for (s, v) in waits:
                    eng_obj.wait_ge(sems[s], v)
                fn(eng_obj).then_inc(sems[inc[0]], inc[1])
            for (e2, s, v) in P.final_waits:
                if e2 == name:
                    eng_obj.wait_ge(sems[s], v)

        @block.tensor
        def _(e):
            replay(e, "pe")

        @block.scalar
        def _(e):
            replay(e, "act")

        @block.vector
        def _(e):
            replay(e, "dve")

        @block.sync
        def _(e):
            replay(e, "sync")

        @block.gpsimd
        def _(e):
            replay(e, "gp")

    return nc


_NC_CACHE = [None]


def kernel(L0_attn_x, L0_attn_Wq, L0_attn_Wk, L0_attn_Wv, L0_attn_Wo,
           L0_ffn_A, L0_ffn_B, L0_ffn_Wr):
    x = np.ascontiguousarray(np.asarray(L0_attn_x).reshape(S_TOT, DM), dtype=np.float32)
    Wq = np.asarray(L0_attn_Wq, dtype=np.float32)
    Wk = np.asarray(L0_attn_Wk, dtype=np.float32)
    Wv = np.asarray(L0_attn_Wv, dtype=np.float32)
    Wo = np.asarray(L0_attn_Wo, dtype=np.float32)
    A = np.asarray(L0_ffn_A, dtype=np.float32)
    B = np.asarray(L0_ffn_B, dtype=np.float32)
    Wr = np.asarray(L0_ffn_Wr, dtype=np.float32)

    wq_arr = np.ascontiguousarray(Wq.reshape(H, 128, 32, 128).transpose(0, 3, 2, 1))
    wo_arr = np.ascontiguousarray(Wo.reshape(8, 512, 8, 4, 128).transpose(0, 2, 4, 3, 1))
    wk_arr = np.ascontiguousarray(Wk.reshape(128, 32, 128).transpose(2, 1, 0))
    wv_arr = np.ascontiguousarray(Wv.reshape(128, 32, 128).transpose(2, 1, 0))
    awr_arr = np.ascontiguousarray(A.reshape(64, 32, 128).transpose(2, 1, 0))
    wr_arr = np.ascontiguousarray(Wr.reshape(4, 32, 128).transpose(2, 1, 0))
    bt_arr = np.ascontiguousarray(B.transpose(0, 2, 1).reshape(64, 8, 512).transpose(1, 0, 2))
    idb = np.eye(128, dtype=np.float32).astype(BF)
    idf = np.eye(128, dtype=np.float32)
    exe = np.zeros((4, 64), dtype=np.float32)
    for e in range(4):
        exe[e, 16 * e:16 * e + 16] = 1.0
    exe = exe.astype(BF)
    ramp = np.linspace(0.0, 1.0, 256, dtype=np.float32)

    in_maps = []
    for c in range(NCORES):
        g0 = T * c
        lo, hi = g0 - 256, g0 + T + 256
        xh = np.zeros((HALO, DM), dtype=np.float32)
        a0, a1 = max(lo, 0), min(hi, S_TOT)
        xh[a0 - lo:a1 - lo] = x[a0:a1]
        xr = np.ascontiguousarray(
            x[g0:g0 + T].reshape(2, 8, 128, 8, 512).transpose(0, 3, 2, 1, 4))
        pos = np.arange(lo, hi)
        cosc, sinc = rope_cache(pos)
        cosT = np.ascontiguousarray(cosc.T * SC_HALF).astype(BF)
        sinneg = sinc.T * SC_HALF
        sinneg[0:64, :] = -sinneg[0:64, :]
        sinT = np.ascontiguousarray(sinneg).astype(BF)
        alf = np.zeros((128, 16), dtype=np.float32)
        omaa = np.zeros((128, 16), dtype=np.float32)
        corr = np.zeros((128, 16), dtype=np.float32)
        for sg in range(8):
            for qt in range(2):
                col = 2 * sg + qt
                av = ramp[128 * qt:128 * qt + 128]
                if c == 0 and sg == 0:
                    alf[:, col] = 1.0
                    omaa[:, col] = 0.0
                else:
                    alf[:, col] = av
                    omaa[:, col] = 1.0 - av
                if c == NCORES - 1 and sg == 7:
                    corr[:, col] = 128.0
        in_maps.append({
            "xh": xh, "xr": xr, "wq": wq_arr, "wo": wo_arr, "wk": wk_arr,
            "wv": wv_arr, "awr": awr_arr, "wr": wr_arr, "bt": bt_arr,
            "cosT": cosT, "sinT": sinT, "alf": alf, "oma": omaa, "cor": corr,
            "idb": idb, "idf": idf, "exe": exe,
        })

    if _NC_CACHE[0] is None:
        _NC_CACHE[0] = build_program()
    res = run_bass_kernel_spmd(_NC_CACHE[0], in_maps, core_ids=list(range(NCORES)))
    out = np.concatenate([r["y"] for r in res.results], axis=0)
    return out.reshape(1, S_TOT, DM).astype(np.float32)


# version 07 (final)
# speedup vs baseline: 1.0988x; 1.0988x over previous
import numpy as np
import ml_dtypes
import concourse.bass as bass
import concourse.mybir as mybir
from concourse.bass_utils import run_bass_kernel_spmd

F32 = mybir.dt.float32
BF16 = mybir.dt.bfloat16
AF = mybir.ActivationFunctionType
ALU = mybir.AluOpType
AX = mybir.AxisListType
BF = ml_dtypes.bfloat16

H, DH, DM = 32, 128, 4096
S_TOT = 16384
NCORES = 8
T = 2048
HALO = 2560
CH = 1024
NCHUNK = 2
ROPE_BASE = 10000.0
SC_HALF = float((1.0 / np.sqrt(DH)) ** 0.5)
EPS = 1e-5

ENGS = ("pe", "act", "dve", "sync", "gp")
DEBUG = False


class Prog:
    def __init__(self):
        self.ops = {e: [] for e in ENGS}
        self.cnt = {e: 0 for e in ENGS}
        self.dma_cnt = {"sync": 0, "gp": 0}
        self.last_wait = {e: {} for e in ENGS}
        self.final_waits = []

    def _avail(self, sem):
        if sem == "syd":
            return 16 * self.dma_cnt["sync"]
        if sem == "gpd":
            return 16 * self.dma_cnt["gp"]
        return self.cnt[sem]

    def emit(self, eng, fn, deps=()):
        waits = []
        for d in deps:
            if d is None:
                continue
            sem, val = d
            if sem == eng:
                continue
            assert val <= self._avail(sem), f"dep on future: {sem} {val} > {self._avail(sem)}"
            lw = self.last_wait[eng]
            if lw.get(sem, -1) >= val:
                continue
            lw[sem] = val
            waits.append((sem, val))
        if eng in ("sync", "gp"):
            self.dma_cnt[eng] += 1
            dsem = "syd" if eng == "sync" else "gpd"
            self.ops[eng].append((waits, fn, (dsem, 16)))
            return (dsem, 16 * self.dma_cnt[eng])
        self.cnt[eng] += 1
        self.ops[eng].append((waits, fn, (eng, 1)))
        return (eng, self.cnt[eng])

    def join(self):
        return [("pe", self.cnt["pe"]), ("act", self.cnt["act"]), ("dve", self.cnt["dve"]),
                ("syd", 16 * self.dma_cnt["sync"]), ("gpd", 16 * self.dma_cnt["gp"])]


def ecopy(e, out, in_):
    f = getattr(e, "tensor_copy", None)
    if f is None:
        return e.copy(out, in_)
    return f(out, in_)


def rope_cache(positions):
    inv_freq = 1.0 / ROPE_BASE ** (np.arange(0, DH, 2, dtype=np.float32) / DH)
    f = np.outer(positions.astype(np.float32), inv_freq)
    cos = np.zeros((len(positions), DH), dtype=np.float32)
    sin = np.zeros((len(positions), DH), dtype=np.float32)
    cos[:, 0::2] = np.cos(f); cos[:, 1::2] = np.cos(f)
    sin[:, 0::2] = np.sin(f); sin[:, 1::2] = np.sin(f)
    return cos, sin


def build_program():
    nc = bass.Bass("TRN2", target_bir_lowering=False)
    d_xh = nc.dram_tensor("xh", [HALO, DM], F32, kind="ExternalInput").ap()
    d_xr = nc.dram_tensor("xr", [NCHUNK, 8, 128, 8, 512], F32, kind="ExternalInput").ap()
    d_wq = nc.dram_tensor("wq", [H, 128, 32, 128], F32, kind="ExternalInput").ap()
    d_wo = nc.dram_tensor("wo", [8, 8, 128, 4, 512], F32, kind="ExternalInput").ap()
    d_wk = nc.dram_tensor("wk", [128, 32, 128], F32, kind="ExternalInput").ap()
    d_wv = nc.dram_tensor("wv", [128, 32, 128], F32, kind="ExternalInput").ap()
    d_awr = nc.dram_tensor("awr", [128, 32, 64], F32, kind="ExternalInput").ap()
    d_wr = nc.dram_tensor("wr", [128, 32, 4], F32, kind="ExternalInput").ap()
    d_bt = nc.dram_tensor("bt", [8, 64, 512], F32, kind="ExternalInput").ap()
    d_cos = nc.dram_tensor("cosT", [128, HALO], BF16, kind="ExternalInput").ap()
    d_sin = nc.dram_tensor("sinT", [128, HALO], BF16, kind="ExternalInput").ap()
    d_alf = nc.dram_tensor("alf", [128, 16], F32, kind="ExternalInput").ap()
    d_oma = nc.dram_tensor("oma", [128, 16], F32, kind="ExternalInput").ap()
    d_cor = nc.dram_tensor("cor", [128, 16], F32, kind="ExternalInput").ap()
    d_idb = nc.dram_tensor("idb", [128, 128], BF16, kind="ExternalInput").ap()
    d_idf = nc.dram_tensor("idf", [128, 128], F32, kind="ExternalInput").ap()
    d_exe = nc.dram_tensor("exe", [4, 64], BF16, kind="ExternalInput").ap()
    d_y = nc.dram_tensor("y", [T, DM], F32, kind="ExternalOutput").ap()
    if DEBUG:
        d_dbg = {
            "dkT": (nc.dram_tensor("dkT", [128, HALO], BF16, kind="ExternalOutput").ap(), None),
            "dvN": (nc.dram_tensor("dvN", [128, 20, 128], BF16, kind="ExternalOutput").ap(), None),
            "dxnT": (nc.dram_tensor("dxnT", [128, 32, CH], BF16, kind="ExternalOutput").ap(), None),
            "datT": (nc.dram_tensor("datT", [128, 32, CH], BF16, kind="ExternalOutput").ap(), None),
            "dmidp": (nc.dram_tensor("dmidp", [64, 1024], BF16, kind="ExternalOutput").ap(), None),
            "dgbf": (nc.dram_tensor("dgbf", [128, 32], BF16, kind="ExternalOutput").ap(), None),
            "dst": (nc.dram_tensor("dst", [128, 8], F32, kind="ExternalOutput").ap(), None),
            "datT0": (nc.dram_tensor("datT0", [128, 32, CH], BF16, kind="ExternalOutput").ap(), None),
            "dxnT0": (nc.dram_tensor("dxnT0", [128, 32, CH], BF16, kind="ExternalOutput").ap(), None),
            "dmidp0": (nc.dram_tensor("dmidp0", [64, 1024], BF16, kind="ExternalOutput").ap(), None),
        }

    P = Prog()

    def sballoc(name, shape, dtype):
        g = nc.sbuf_tensor(name, shape, dtype)
        return g.__enter__(), g

    def psalloc(name, shape, dtype):
        g = nc.psum_tensor(name, shape, dtype)
        return g.__enter__(), g

    xnT, _ = sballoc("xnT", [128, 32, CH], BF16)
    atT, _ = sballoc("atT", [128, 32, CH], BF16)
    kT, _ = sballoc("kT", [128, HALO], BF16)
    vN, _ = sballoc("vN", [128, 20, 128], BF16)
    cosT, _ = sballoc("cosS", [128, HALO], BF16)
    sinT, _ = sballoc("sinS", [128, HALO], BF16)
    wqs = [sballoc(f"wqs{i}", [128, 32, 128], BF16)[0] for i in range(2)]
    awrs, _ = sballoc("awrs", [128, 32, 64], BF16)
    wrs, _ = sballoc("wrs", [128, 32, 4], BF16)
    bts = [sballoc(f"bts{i}", [64, 512], BF16)[0] for i in range(2)]
    idb, _ = sballoc("idbS", [128, 128], BF16)
    idf, _ = sballoc("idfS", [128, 128], F32)
    alf, _ = sballoc("alfS", [128, 16], F32)
    oma, _ = sballoc("omaS", [128, 16], F32)
    cor, _ = sballoc("corS", [128, 16], F32)
    eps_sb, _ = sballoc("eps", [128, 1], F32)
    c_inv, _ = sballoc("cinv", [128, 1], F32)
    c_zero, _ = sballoc("czero", [128, 1], F32)
    c_qtr, _ = sballoc("cqtr", [128, 1], F32)
    dscr, _ = sballoc("dscr", [128, 1], F32)
    exe_sb, _ = sballoc("exeS", [4, 64], BF16)
    xpool = [sballoc(f"xpool{i}", [128, 2048], F32)[0] for i in range(2)]
    xsth = xpool
    rt1 = xpool[0][:, 0:1024]
    rt2 = xpool[0][:, 1024:2048]
    bpool, _ = sballoc("bpool", [128, 4096], BF16)
    xbfp = bpool
    qrb = bpool[:, 0:1024]
    PbA = bpool[:, 1024:2560]
    PbB = bpool[:, 2560:4096]
    gts = bpool[0:4, 0:1024]
    wor = bpool
    gbig = bpool[0:64, 1024:2048]
    kpool, _ = sballoc("kpool", [128, 1024], F32)
    kt1 = kpool[:, 0:512]
    kt2 = kpool[:, 512:1024]
    lg4 = kpool[0:4, 0:1024]
    outb = [kpool[:, 0:512], kpool[:, 512:1024]]
    PTs, _ = sballoc("PTs", [128, 6, 256], BF16)
    midp, _ = sballoc("midp", [64, 1024], BF16)
    gl, _ = sballoc("gl", [128, 32], F32)
    gbf, _ = sballoc("gbf", [128, 32], BF16)
    ge4 = [sballoc(f"ge4{i}", [128, 4], F32)[0] for i in range(2)]
    mk4 = [sballoc(f"mk4{i}", [128, 4], F32)[0] for i in range(2)]
    g4 = [sballoc(f"g4{i}", [128, 4], F32)[0] for i in range(2)]
    stat = {}
    for nm in ("sqh0", "sqh1", "smh0", "smh1", "sm", "mean", "sq", "msq", "var", "sd", "rs", "mrs"):
        stat[nm] = [sballoc(f"{nm}{i}", [128, 1], F32)[0] for i in range(2)]
    asc = {}
    for nm in ("sA0", "sB0", "sC0", "sA1", "sB1", "sC1", "r10", "r20", "r11", "r21",
               "q10", "q20", "q11", "q21", "u0", "v0", "w0", "u1", "v1", "w1"):
        asc[nm] = sballoc("a" + nm, [128, 1], F32)[0]
    gsc = {}
    for nm in ("p1", "p2", "p3", "p4", "t1", "t2", "thr", "m1", "nm1", "z1", "z2", "zz", "wg"):
        gsc[nm] = [sballoc(f"g{nm}{i}", [128, 1], F32)[0] for i in range(2)]

    tcons = {}
    for key, dst, src in (("cos", cosT, d_cos), ("sin", sinT, d_sin), ("alf", alf, d_alf),
                          ("oma", oma, d_oma), ("cor", cor, d_cor), ("idb", idb, d_idb),
                          ("idf", idf, d_idf), ("exe", exe_sb, d_exe)):
        tcons[key] = P.emit("sync", lambda e, a=dst, b=src: e.dma_start(out=a[:], in_=b))
    tok_wk = P.emit("gp", lambda e: e.dma_start(out=wqs[0][:], in_=d_wk))
    tok_wv = P.emit("gp", lambda e: e.dma_start(out=wqs[1][:], in_=d_wv))
    tok_awr = P.emit("gp", lambda e: e.dma_start(out=awrs[:], in_=d_awr))
    tok_wr = P.emit("gp", lambda e: e.dma_start(out=wrs[:], in_=d_wr))
    tok_eps = P.emit("dve", lambda e: e.memset(eps_sb[:], EPS))
    tok_eps = P.emit("dve", lambda e: e.memset(c_inv[:], 1.0 / DM), deps=[tok_eps])
    tok_eps = P.emit("dve", lambda e: e.memset(c_zero[:], 0.0), deps=[tok_eps])
    tok_eps = P.emit("dve", lambda e: e.memset(c_qtr[:], 0.25), deps=[tok_eps])

    def ln_tile(r0, tcol, par, lnt, prev_par, xbf_war, prev_ev, extra_load_deps):
        s = stat
        t_ld = []
        for j in range(2):
            t_ld.append(P.emit("sync", lambda e, j=j: e.dma_start(
                out=xsth[j][:], in_=d_xh[r0:r0 + 128, 2048 * j:2048 * j + 2048]),
                deps=list(prev_par.get(j, [])) + list(extra_load_deps)))
        t_sq, t_sm = [], []
        for j in range(2):
            t_sq.append(P.emit("act", lambda e, j=j: e.activation(
                xbfp[:, 2048 * j:2048 * j + 2048], xsth[j][:], AF.Square,
                accum_out=s["sqh%d" % j][par][:]), deps=[t_ld[j]] + list(xbf_war)))
            t_sm.append(P.emit("act", lambda e, j=j: e.activation(
                xbfp[:, 2048 * j:2048 * j + 2048], xsth[j][:], AF.Copy,
                accum_out=s["smh%d" % j][par][:]), deps=[t_ld[j]] + list(xbf_war)))
        t1 = P.emit("dve", lambda e: e.tensor_add(s["sm"][par][:], s["smh0"][par][:], s["smh1"][par][:]), deps=[t_sm[0], t_sm[1]])
        t3 = P.emit("dve", lambda e: e.tensor_add(s["sq"][par][:], s["sqh0"][par][:], s["sqh1"][par][:]), deps=[t_sq[0], t_sq[1]])
        t2 = P.emit("dve", lambda e: e.tensor_mul(s["mean"][par][:], s["sm"][par][:], c_inv[:]), deps=[t1, tok_eps])
        t5 = P.emit("dve", lambda e: e.tensor_mul(s["var"][par][:], s["sq"][par][:], c_inv[:]), deps=[t3, tok_eps])
        t4 = P.emit("dve", lambda e: e.tensor_mul(s["msq"][par][:], s["mean"][par][:], s["mean"][par][:]), deps=[t2])
        P.emit("dve", lambda e: e.tensor_copy(dscr[:], c_zero[:]), deps=[tok_eps])
        t6 = P.emit("dve", lambda e: e.tensor_sub(s["var"][par][:], s["var"][par][:], s["msq"][par][:]), deps=[t4, t5])
        t7 = P.emit("act", lambda e: e.activation(s["sd"][par][:], s["var"][par][:], AF.Sqrt, bias=eps_sb[:], scale=1.0), deps=[t6, tok_eps])
        t7b = P.emit("dve", lambda e: e.reciprocal(s["rs"][par][:], s["sd"][par][:]), deps=[t7])
        P.emit("dve", lambda e: e.tensor_copy(dscr[:], c_zero[:]), deps=[tok_eps])
        t_nm = []
        for j in range(2):
            t_nm.append(P.emit("dve", lambda e, j=j: e.tensor_scalar(
                out=xbfp[:, 2048 * j:2048 * j + 2048], in0=xsth[j][:],
                scalar1=s["mean"][par][:], scalar2=s["rs"][par][:],
                op0=ALU.subtract, op1=ALU.mult), deps=[t7b, t2] + list(xbf_war)))
        ev_toks = []
        for bi in range(8):
            lp = bi % 2
            tps = None
            dd0 = [t_nm[0] if 4 * bi < 16 else t_nm[1], t_nm[1] if 4 * bi + 3 >= 16 else None,
                   tcons["idb"], prev_ev.get(lp)]
            for k in range(4):
                c = 4 * bi + k
                tps = P.emit("pe", lambda e, c=c, k=k, lp=lp: e.matmul(
                    lnt[lp][:, k, :], xbfp[:, 128 * c:128 * c + 128], idb[:],
                    is_transpose=True, start=(k == 0), stop=(k == 3)),
                    deps=dd0 if k == 0 else [])
            eng = "act" if bi % 2 == 0 else "dve"
            ev = P.emit(eng, lambda e, bi=bi, lp=lp: ecopy(e,
                xnT[:, 4 * bi:4 * bi + 4, tcol:tcol + 128], lnt[lp][:]), deps=[tps])
            prev_ev[lp] = ev
            ev_toks.append(ev)
        prev_par[0] = [t_nm[0], t_sm[0], t_sq[0]]
        prev_par[1] = [t_nm[1], t_sm[1], t_sq[1]]
        return t_nm, ev_toks, tps

    lnt = [psalloc(f"lnp{i}", [128, 4, 128], BF16) for i in range(2)]
    kp, kp_g = psalloc("kp", [128, 512], F32)
    vp, vp_g = psalloc("vp", [128, 128], F32)

    prev_par = {}
    prev_ev = {0: None, 1: None}
    prev_k_rope = None
    prev_v_ev = None
    last_tp = None
    for kc in range(5):
        evs = []
        for t2 in range(4):
            t_nm, ev_toks, last_tp = ln_tile(512 * kc + 128 * t2, 128 * t2, t2 % 2,
                                             [lnt[0][0], lnt[1][0]], prev_par,
                                             [last_tp] if last_tp else [], prev_ev, [])
            evs.extend(ev_toks)
        dk = evs + [tok_wk]
        if prev_k_rope is not None:
            dk.append(prev_k_rope)
        tkm = None
        for c in range(32):
            tkm = P.emit("pe", lambda e, c=c: e.matmul(
                kp[:], wqs[0][:, c, :], xnT[:, c, 0:512], start=(c == 0), stop=(c == 31)),
                deps=dk if c == 0 else [])
        co = 512 * kc
        tr1 = P.emit("dve", lambda e, co=co: e.tensor_mul(kt1[:], kp[:], cosT[:, co:co + 512]), deps=[tkm, tcons["cos"]])
        tr2 = P.emit("dve", lambda e, co=co: e.tensor_mul(kt2[0:64, :], kp[64:128, :], sinT[0:64, co:co + 512]), deps=[tkm, tcons["sin"]])
        tr3 = P.emit("dve", lambda e, co=co: e.tensor_mul(kt2[64:128, :], kp[0:64, :], sinT[64:128, co:co + 512]), deps=[tkm])
        prev_k_rope = P.emit("dve", lambda e, co=co: e.tensor_add(kT[:, co:co + 512], kt1[:], kt2[:]), deps=[tr1, tr2, tr3])
        for t2 in range(4):
            dv = [evs[-1], tok_wv]
            if prev_v_ev is not None:
                dv.append(prev_v_ev)
            tvm = None
            for c in range(32):
                tvm = P.emit("pe", lambda e, c=c, t2=t2: e.matmul(
                    vp[:], xnT[:, c, 128 * t2:128 * t2 + 128], wqs[1][:, c, :],
                    start=(c == 0), stop=(c == 31)), deps=dv if c == 0 else [])
            prev_v_ev = P.emit("act", lambda e, kc=kc, t2=t2: ecopy(e,
                vN[:, 4 * kc + t2, :], vp[:]), deps=[tvm])

    vp_g.__exit__(None, None, None)
    kp_g.__exit__(None, None, None)
    lnt[1][1].__exit__(None, None, None)
    lnt[0][1].__exit__(None, None, None)

    last_join = P.join()
    tok_y_final = []

    for ch in range(NCHUNK):
        lnt = [psalloc(f"ln{ch}{i}", [128, 4, 128], BF16) for i in range(2)]
        ln_join = last_join
        tok_wq = [None, None]
        tok_wq[0] = P.emit("gp", lambda e: e.dma_start(out=wqs[0][:], in_=d_wq[0]), deps=ln_join)
        tok_wq[1] = P.emit("gp", lambda e: e.dma_start(out=wqs[1][:], in_=d_wq[1]), deps=ln_join)
        prev_par = dict(prev_par)
        first = True
        for t in range(8):
            extra = list(ln_join) if first else []
            t_nm, ev_toks, last_tp = ln_tile(256 + CH * ch + 128 * t, 128 * t, t % 2,
                                             [lnt[0][0], lnt[1][0]], prev_par,
                                             ([last_tp] if last_tp else []) + (list(ln_join) if first else []),
                                             prev_ev, extra)
            first = False
        lnt[1][1].__exit__(None, None, None)
        lnt[0][1].__exit__(None, None, None)
        xnT_ready = P.join()

        qp, qp_g = psalloc(f"qps{ch}", [128, 1024], F32)
        smp, smp_g = psalloc(f"smps{ch}", [128, 1024], F32)
        stp, stp_g = psalloc(f"stps{ch}", [128, 512], F32)
        pt0, pt0_g = psalloc(f"ptps0{ch}", [128, 6, 128], BF16)
        pt1, pt1_g = psalloc(f"ptps1{ch}", [128, 6, 128], BF16)
        avp, avp_g = psalloc(f"avps{ch}", [128, 256], F32)

        qoff = 256 + CH * ch
        st_attn = {
            "sig_prev": None,
            "rm_prev": {0: None, 1: None},
            "sc_prev": {0: None, 1: None},
            "pt_prev": {0: None, 1: None},
            "ptev_prev": {0: None, 1: None},
            "av_prev": None,
            "avev_prev": None,
            "rope": None,
        }
        war_atT = list(ln_join)

        def emit_q(h, extra):
            toks = []
            for nh in range(2):
                dd = [tok_wq[h % 2]] + xnT_ready + list(extra) + [st_attn["rope"]]
                tq = None
                for c in range(32):
                    tq = P.emit("pe", lambda e, c=c, nh=nh, h=h: e.matmul(
                        qp[:, 512 * nh:512 * nh + 512], wqs[h % 2][:, c, :],
                        xnT[:, c, 512 * nh:512 * nh + 512],
                        start=(c == 0), stop=(c == 31)), deps=dd if c == 0 else [])
                o = qoff + 512 * nh
                q0 = 512 * nh
                ta = P.emit("dve", lambda e, o=o, q0=q0: e.tensor_mul(
                    rt1[:, q0:q0 + 512], qp[:, q0:q0 + 512], cosT[:, o:o + 512]), deps=[tq, tcons["cos"]])
                tb = P.emit("dve", lambda e, o=o, q0=q0: e.tensor_mul(
                    rt2[0:64, q0:q0 + 512], qp[64:128, q0:q0 + 512], sinT[0:64, o:o + 512]), deps=[tq, tcons["sin"]])
                tc = P.emit("dve", lambda e, o=o, q0=q0: e.tensor_mul(
                    rt2[64:128, q0:q0 + 512], qp[0:64, q0:q0 + 512], sinT[64:128, o:o + 512]), deps=[tq])
                td = P.emit("dve", lambda e, q0=q0: e.tensor_add(
                    qrb[:, q0:q0 + 512], rt1[:, q0:q0 + 512], rt2[:, q0:q0 + 512]), deps=[ta, tb, tc] + list(extra))
                toks.append(td)
            st_attn["rope"] = toks[1]

        def seg_S(s4):
            sg = 4 * ch + s4
            klo = 256 * sg
            sp = st_attn["sig_prev"]
            res = {}
            for qt in range(2):
                dd = [st_attn["rope"], prev_k_rope]
                if sp is not None:
                    dd.append(sp[qt])
                res[qt] = P.emit("pe", lambda e, qt=qt, klo=klo, s4=s4: e.matmul(
                    smp[:, 512 * qt:512 * qt + 512],
                    qrb[:, 256 * s4 + 128 * qt:256 * s4 + 128 * qt + 128],
                    kT[:, klo:klo + 512], start=True, stop=True), deps=dd)
            ddt = [st_attn["rope"], prev_k_rope]
            if sp is not None:
                ddt.append(sp[2])
            tt0 = P.emit("pe", lambda e, klo=klo, s4=s4: e.matmul(
                stp[:, 0:256], qrb[:, 256 * s4:256 * s4 + 128],
                kT[:, klo + 512:klo + 768], start=True, stop=False), deps=ddt)
            res["t"] = P.emit("pe", lambda e, klo=klo, s4=s4: e.matmul(
                stp[:, 256:512], qrb[:, 256 * s4 + 128:256 * s4 + 256],
                kT[:, klo + 512:klo + 768], start=False, stop=True), deps=[tt0])
            return res

        def seg_post(s4, S):
            sg = 4 * ch + s4
            col = 2 * sg
            sig = {}
            sc_out = []
            for qt in range(2):
                scp = st_attn["sc_prev"][qt]
                rmp = st_attn["rm_prev"][qt]
                war = list(scp) if scp else []
                warr = list(rmp) if rmp else []
                sig[f"A{qt}"] = P.emit("act", lambda e, qt=qt: e.activation(
                    PbA[:, 768 * qt:768 * qt + 256], smp[:, 512 * qt:512 * qt + 256], AF.Sigmoid,
                    accum_out=asc[f"sA{qt}"][:]), deps=[S[qt]] + war[0:1] + warr)
                sig[f"B{qt}"] = P.emit("act", lambda e, qt=qt: e.activation(
                    PbA[:, 768 * qt + 256:768 * qt + 512], smp[:, 512 * qt + 256:512 * qt + 512], AF.Sigmoid,
                    accum_out=asc[f"sB{qt}"][:]), deps=[S[qt]] + war[1:2])
                sig[f"C{qt}"] = P.emit("act", lambda e, qt=qt: e.activation(
                    PbA[:, 768 * qt + 512:768 * qt + 768], stp[:, 256 * qt:256 * qt + 256], AF.Sigmoid,
                    accum_out=asc[f"sC{qt}"][:]), deps=[S["t"]] + war[2:3])
            cc = {}
            for step in range(8):
                for qt in range(2):
                    a = asc
                    r1, r2 = a[f"r1{qt}"], a[f"r2{qt}"]
                    q1, q2 = a[f"q1{qt}"], a[f"q2{qt}"]
                    u, v, w = a[f"u{qt}"], a[f"v{qt}"], a[f"w{qt}"]
                    if step == 0:
                        cc[(1, qt)] = P.emit("dve", lambda e, qt=qt, r1=r1: e.tensor_add(r1[:], asc[f"sA{qt}"][:], asc[f"sB{qt}"][:]), deps=[sig[f"A{qt}"], sig[f"B{qt}"]])
                    elif step == 1:
                        cc[(2, qt)] = P.emit("dve", lambda e, qt=qt, r2=r2: e.tensor_add(r2[:], asc[f"sB{qt}"][:], asc[f"sC{qt}"][:]), deps=[sig[f"B{qt}"], sig[f"C{qt}"]])
                    elif step == 2:
                        cc[(3, qt)] = P.emit("dve", lambda e, r2=r2, col=col, qt=qt: e.tensor_sub(r2[:], r2[:], cor[:, col + qt:col + qt + 1]), deps=[cc[(2, qt)], tcons["cor"]])
                    elif step == 3:
                        cc[(4, qt)] = P.emit("dve", lambda e, r1=r1, q1=q1: e.reciprocal(q1[:], r1[:]), deps=[cc[(1, qt)]])
                    elif step == 4:
                        cc[(5, qt)] = P.emit("dve", lambda e, r2=r2, q2=q2: e.reciprocal(q2[:], r2[:]), deps=[cc[(3, qt)]])
                    elif step == 5:
                        cc[(6, qt)] = P.emit("dve", lambda e, q1=q1, u=u, col=col, qt=qt: e.tensor_mul(u[:], oma[:, col + qt:col + qt + 1], q1[:]), deps=[cc[(4, qt)], tcons["oma"]])
                    elif step == 6:
                        cc[(7, qt)] = P.emit("dve", lambda e, q2=q2, v=v, col=col, qt=qt: e.tensor_mul(v[:], alf[:, col + qt:col + qt + 1], q2[:]), deps=[cc[(5, qt)], tcons["alf"]])
                    elif step == 7:
                        cc[(8, qt)] = P.emit("dve", lambda e, u=u, v=v, w=w: e.tensor_add(w[:], u[:], v[:]), deps=[cc[(6, qt)], cc[(7, qt)]])
            for qt in range(2):
                a = asc
                u, v, w = a[f"u{qt}"], a[f"v{qt}"], a[f"w{qt}"]
                c6, c7, c8 = cc[(6, qt)], cc[(7, qt)], cc[(8, qt)]
                ptw = st_attn["pt_prev"][qt]
                s1 = P.emit("dve", lambda e, qt=qt, u=u: e.tensor_scalar(
                    out=PbB[:, 768 * qt:768 * qt + 256], in0=PbA[:, 768 * qt:768 * qt + 256],
                    scalar1=u[:], scalar2=None, op0=ALU.mult), deps=[c6, ptw])
                s2 = P.emit("dve", lambda e, qt=qt, w=w: e.tensor_scalar(
                    out=PbB[:, 768 * qt + 256:768 * qt + 512], in0=PbA[:, 768 * qt + 256:768 * qt + 512],
                    scalar1=w[:], scalar2=None, op0=ALU.mult), deps=[c8])
                s3 = P.emit("dve", lambda e, qt=qt, v=v: e.tensor_scalar(
                    out=PbB[:, 768 * qt + 512:768 * qt + 768], in0=PbA[:, 768 * qt + 512:768 * qt + 768],
                    scalar1=v[:], scalar2=None, op0=ALU.mult), deps=[c7])
                st_attn["sc_prev"][qt] = (s1, s2, s3)
                st_attn["rm_prev"][qt] = [cc[(1, qt)], cc[(2, qt)]]
                sc_out.append((s1, s2, s3))
            st_attn["sig_prev"] = (
                max(sig["A0"], sig["B0"], key=lambda t: t[1]),
                max(sig["A1"], sig["B1"], key=lambda t: t[1]),
                max(sig["C0"], sig["C1"], key=lambda t: t[1]),
            )
            return sc_out

        def seg_ptav(h, s4, sc):
            sg = 4 * ch + s4
            ptev = []
            for qt in range(2):
                pt = pt0 if qt == 0 else pt1
                dd0 = list(sc[qt]) + [tcons["idb"], st_attn["ptev_prev"][qt]]
                tps = None
                for j in range(6):
                    tps = P.emit("pe", lambda e, qt=qt, j=j, pt=pt: e.matmul(
                        pt[:, j, :], PbB[:, 768 * qt + 128 * j:768 * qt + 128 * j + 128], idb[:],
                        is_transpose=True, start=(j == 0), stop=(j == 5)),
                        deps=dd0 if j == 0 else [])
                st_attn["pt_prev"][qt] = tps
                eng = "act" if qt == 0 else "dve"
                ev = P.emit(eng, lambda e, qt=qt, pt=pt: ecopy(e,
                    PTs[:, :, 128 * qt:128 * qt + 128], pt[:]),
                    deps=[tps, st_attn["av_prev"]])
                st_attn["ptev_prev"][qt] = ev
                ptev.append(ev)
            ddav = ptev + [st_attn["avev_prev"]]
            tav = None
            for j in range(6):
                tav = P.emit("pe", lambda e, j=j, sg=sg: e.matmul(
                    avp[:], vN[:, 2 * sg + j, :], PTs[:, j, :], start=(j == 0), stop=(j == 5)),
                    deps=ddav if j == 0 else [])
            st_attn["av_prev"] = tav
            aev = P.emit("dve", lambda e, h=h, s4=s4: e.tensor_copy(
                atT[:, h, 256 * s4:256 * s4 + 256], avp[:]), deps=[tav] + war_atT)
            st_attn["avev_prev"] = aev
            return aev

        emit_q(0, xnT_ready)
        last_aev = None
        for h in range(H):
            S0 = seg_S(0); sc0 = seg_post(0, S0)
            S1 = seg_S(1); seg_ptav(h, 0, sc0); sc1 = seg_post(1, S1)
            S2 = seg_S(2); seg_ptav(h, 1, sc1); sc2 = seg_post(2, S2)
            S3 = seg_S(3); seg_ptav(h, 2, sc2); sc3 = seg_post(3, S3)
            if h + 1 < H:
                tok_wq[(h + 1) % 2] = P.emit("gp", lambda e, h=h: e.dma_start(
                    out=wqs[(h + 1) % 2][:], in_=d_wq[h + 1]), deps=[S0[0]])
                emit_q(h + 1, [S3["t"]])
            last_aev = seg_ptav(h, 3, sc3)

        attn_join = P.join()
        for g in (avp_g, pt1_g, pt0_g, stp_g, smp_g, qp_g):
            g.__exit__(None, None, None)

        mid, mid_g = psalloc(f"midps{ch}", [64, 1024], F32)
        gtp, gtp_g = psalloc(f"gtps{ch}", [4, 1024], BF16)
        lgp, lgp_g = psalloc(f"lgps{ch}", [4, 1024], F32)
        t4p, t4p_g = psalloc(f"t4ps{ch}", [128, 32], F32)

        t_mid = []
        for nh in range(2):
            tm = None
            for c in range(32):
                tm = P.emit("pe", lambda e, c=c, nh=nh: e.matmul(
                    mid[:, 512 * nh:512 * nh + 512], awrs[:, c, :],
                    xnT[:, c, 512 * nh:512 * nh + 512], start=(c == 0), stop=(c == 31)),
                    deps=(attn_join + [tok_awr]) if c == 0 else [])
            t_mid.append(tm)
        t_lg = []
        for nh in range(2):
            tm = None
            for c in range(32):
                tm = P.emit("pe", lambda e, c=c, nh=nh: e.matmul(
                    lgp[:, 512 * nh:512 * nh + 512], wrs[:, c, :],
                    xnT[:, c, 512 * nh:512 * nh + 512], start=(c == 0), stop=(c == 31)),
                    deps=[tok_wr] if c == 0 else [])
            t_lg.append(tm)
        t_lgev = P.emit("act", lambda e: e.copy(lg4[:], lgp[:]), deps=t_lg)
        tt4 = None
        for t in range(8):
            tt4 = P.emit("pe", lambda e, t=t: e.matmul(
                t4p[:, 4 * t:4 * t + 4], lg4[0:4, 128 * t:128 * t + 128], idf[0:4, 0:4],
                is_transpose=True, start=(t == 0), stop=(t == 7)), deps=[t_lgev, tcons["idf"]])
        t_glev = P.emit("act", lambda e: e.copy(gl[:], t4p[:]), deps=[tt4])
        g_toks = [None] * 8
        bb = {}
        for pair in range(4):
            for step in range(18):
                for q in range(2):
                    t = 2 * pair + q
                    G = lambda nm: gsc[nm][q]
                    d_prev = [g_toks[t - 2]] if t >= 2 else []
                    if step == 0:
                        bb[(1, q)] = P.emit("dve", lambda e, t=t, q=q: e.tensor_max(gsc["p1"][q][:], gl[:, 4 * t:4 * t + 1], gl[:, 4 * t + 1:4 * t + 2]), deps=[t_glev] + d_prev)
                    elif step == 1:
                        bb[(2, q)] = P.emit("dve", lambda e, t=t, q=q: e.tensor_tensor(gsc["p2"][q][:], gl[:, 4 * t:4 * t + 1], gl[:, 4 * t + 1:4 * t + 2], op=ALU.min), deps=[])
                    elif step == 2:
                        bb[(3, q)] = P.emit("dve", lambda e, t=t, q=q: e.tensor_max(gsc["p3"][q][:], gl[:, 4 * t + 2:4 * t + 3], gl[:, 4 * t + 3:4 * t + 4]), deps=[])
                    elif step == 3:
                        bb[(4, q)] = P.emit("dve", lambda e, t=t, q=q: e.tensor_tensor(gsc["p4"][q][:], gl[:, 4 * t + 2:4 * t + 3], gl[:, 4 * t + 3:4 * t + 4], op=ALU.min), deps=[])
                    elif step == 4:
                        bb[(5, q)] = P.emit("dve", lambda e, q=q: e.tensor_tensor(gsc["t1"][q][:], gsc["p1"][q][:], gsc["p3"][q][:], op=ALU.min), deps=[bb[(1, q)], bb[(3, q)]])
                    elif step == 5:
                        bb[(6, q)] = P.emit("dve", lambda e, q=q: e.tensor_max(gsc["t2"][q][:], gsc["p2"][q][:], gsc["p4"][q][:]), deps=[bb[(2, q)], bb[(4, q)]])
                    elif step == 6:
                        bb[(8, q)] = P.emit("dve", lambda e, q=q: e.tensor_max(gsc["m1"][q][:], gsc["p1"][q][:], gsc["p3"][q][:]), deps=[bb[(1, q)], bb[(3, q)]])
                    elif step == 7:
                        bb[(7, q)] = P.emit("dve", lambda e, q=q: e.tensor_max(gsc["thr"][q][:], gsc["t1"][q][:], gsc["t2"][q][:]), deps=[bb[(5, q)], bb[(6, q)]])
                    elif step == 8:
                        bb[(9, q)] = P.emit("dve", lambda e, q=q: e.tensor_sub(gsc["nm1"][q][:], c_zero[:], gsc["m1"][q][:]), deps=[bb[(8, q)], tok_eps])
                    elif step == 9:
                        bb[(11, q)] = P.emit("dve", lambda e, t=t, q=q: e.tensor_scalar(out=mk4[q][:], in0=gl[:, 4 * t:4 * t + 4], scalar1=gsc["thr"][q][:], scalar2=None, op0=ALU.is_ge), deps=[bb[(7, q)]])
                    elif step == 10:
                        bb[(10, q)] = P.emit("act", lambda e, t=t, q=q: e.activation(ge4[q][:], gl[:, 4 * t:4 * t + 4], AF.Exp, bias=gsc["nm1"][q][:], scale=1.0), deps=[bb[(9, q)]])
                    elif step == 11:
                        bb[(12, q)] = P.emit("dve", lambda e, q=q: e.tensor_mul(g4[q][:], ge4[q][:], mk4[q][:]), deps=[bb[(10, q)], bb[(11, q)]])
                    elif step == 12:
                        bb[(13, q)] = P.emit("dve", lambda e, q=q: e.tensor_add(gsc["z1"][q][:], g4[q][:, 0:1], g4[q][:, 1:2]), deps=[bb[(12, q)]])
                    elif step == 13:
                        bb[(14, q)] = P.emit("dve", lambda e, q=q: e.tensor_add(gsc["z2"][q][:], g4[q][:, 2:3], g4[q][:, 3:4]), deps=[bb[(12, q)]])
                    elif step == 14:
                        bb[(15, q)] = P.emit("dve", lambda e, q=q: e.tensor_add(gsc["zz"][q][:], gsc["z1"][q][:], gsc["z2"][q][:]), deps=[bb[(13, q)], bb[(14, q)]])
                    elif step == 15:
                        bb[(16, q)] = P.emit("dve", lambda e, q=q: e.tensor_mul(gsc["t2"][q][:], gsc["zz"][q][:], c_qtr[:]), deps=[bb[(15, q)], tok_eps])
                    elif step == 16:
                        bb[(17, q)] = P.emit("dve", lambda e, q=q: e.reciprocal(gsc["wg"][q][:], gsc["t2"][q][:]), deps=[bb[(16, q)]])
                    elif step == 17:
                        g_toks[t] = P.emit("dve", lambda e, t=t, q=q: e.tensor_scalar(out=gbf[:, 4 * t:4 * t + 4], in0=g4[q][:], scalar1=gsc["wg"][q][:], scalar2=None, op0=ALU.mult), deps=[bb[(17, q)], bb[(12, q)]])
        tgt = None
        for t in range(8):
            tgt = P.emit("pe", lambda e, t=t: e.matmul(
                gtp[:, 128 * t:128 * t + 128], gbf[:, 4 * t:4 * t + 4], idb[:],
                is_transpose=True, start=(t == 0), stop=(t == 7)),
                deps=[g_toks[t], tcons["idb"]])
        t_gts = P.emit("act", lambda e: e.copy(gts[:], gtp[:]), deps=[tgt])
        t4p_g.__exit__(None, None, None)
        lgp_g.__exit__(None, None, None)
        gexp, gexp_g = psalloc(f"gxps{ch}", [64, 1024], F32)
        t_ge = []
        for nh in range(2):
            t_ge.append(P.emit("pe", lambda e, nh=nh: e.matmul(
                gexp[:, 512 * nh:512 * nh + 512], exe_sb[:], gts[0:4, 512 * nh:512 * nh + 512],
                start=True, stop=True), deps=[t_gts, tcons["exe"]]))
        t_gev = P.emit("act", lambda e: e.copy(gbig, gexp[:]), deps=t_ge)
        t_midp = P.emit("dve", lambda e: e.tensor_mul(midp[:], mid[:], gbig), deps=[t_gev] + t_mid)
        ffn_join = P.join()
        for g in (gexp_g, gtp_g, mid_g):
            g.__exit__(None, None, None)

        if DEBUG and ch == 0:
            P.emit("gp", lambda e: e.dma_start(out=d_dbg["datT0"][0], in_=atT[:]), deps=ffn_join)
            P.emit("gp", lambda e: e.dma_start(out=d_dbg["dxnT0"][0], in_=xnT[:]), deps=ffn_join)
            P.emit("gp", lambda e: e.dma_start(out=d_dbg["dmidp0"][0], in_=midp[:]), deps=ffn_join)
        otp, otg = [], []
        for t in range(8):
            h_, g_ = psalloc(f"ops{ch}{t}", [128, 512], F32)
            otp.append(h_)
            otg.append(g_)

        drain_done = {}
        out_dma = {}
        wo_units = {}
        unit_consumed = {}
        bt_tok = {}
        bt_used = {}
        xr_tok = {}
        for s in range(8):
            xr_tok[s] = []
            for hf in range(2):
                dxr = list(ffn_join) if s < 1 else [drain_done[(s - 1, t)] for t in range(4 * hf, 4 * hf + 4)]
                xr_tok[s].append(P.emit("sync", lambda e, s=s, hf=hf, ch=ch: e.dma_start(
                    out=xpool[hf][:], in_=d_xr[ch, s, :, 4 * hf:4 * hf + 4, :]), deps=dxr))
            dbt = list(ffn_join) if s < 2 else [bt_used[s - 2]]
            bt_tok[s] = P.emit("gp", lambda e, s=s: e.dma_start(
                out=bts[s % 2][:], in_=d_bt[s]), deps=dbt)
            for u in range(2):
                k = s * 8 + u
                du = list(ffn_join) if k < 2 else [unit_consumed[divmod(k - 2, 8)]]
                wo_units[(s, u)] = P.emit("gp", lambda e, s=s, u=u: e.dma_start(
                    out=wor[:, 2048 * (u % 2):2048 * (u % 2) + 2048], in_=d_wo[s, u]), deps=du)
            tlast = {}
            for d in range(32):
                u = d // 4
                for t in range(8):
                    dd = []
                    if d == 0:
                        dd = [wo_units[(s, u)], last_aev] + attn_join + list(ffn_join)
                        if s > 0:
                            dd.append(drain_done[(s - 1, t)])
                    elif d % 4 == 0 and t == 0:
                        dd = [wo_units[(s, u)]]
                    tlast[t] = P.emit("pe", lambda e, d=d, t=t, s=s, ot=otp[t]: e.matmul(
                        ot[:], atT[:, d, 128 * t:128 * t + 128],
                        wor[:, 512 * (4 * ((d // 4) % 2) + d % 4):512 * (4 * ((d // 4) % 2) + d % 4) + 512],
                        start=(d == 0), stop=False),
                        deps=dd)
                if d % 4 == 3:
                    unit_consumed[(s, u)] = tlast[7]
                    if u + 2 <= 7:
                        wo_units[(s, u + 2)] = P.emit("gp", lambda e, s=s, u=u: e.dma_start(
                            out=wor[:, 2048 * (u % 2):2048 * (u % 2) + 2048], in_=d_wo[s, u + 2]),
                            deps=[unit_consumed[(s, u)]])
            for t in range(8):
                tlast[t] = P.emit("pe", lambda e, t=t, s=s, ot=otp[t]: e.matmul(
                    ot[:], midp[:, 128 * t:128 * t + 128], bts[s % 2][:],
                    start=False, stop=True), deps=[bt_tok[s], t_midp, tlast[t]])
            bt_used[s] = tlast[7]
            for t in range(8):
                dd = [tlast[t], xr_tok[s][t // 4]]
                kprev = s * 8 + t - 2
                if kprev >= 0:
                    dd.append(out_dma[divmod(kprev, 8)])
                drain_done[(s, t)] = P.emit("dve", lambda e, s=s, t=t, ot=otp[t]: e.tensor_add(
                    outb[t % 2][:], ot[:], xpool[t // 4][:, 512 * (t % 4):512 * (t % 4) + 512]), deps=dd)
                r0 = CH * ch + 128 * t
                tok = P.emit("sync", lambda e, s=s, t=t, r0=r0: e.dma_start(
                    out=d_y[r0:r0 + 128, 512 * s:512 * s + 512], in_=outb[t % 2][:]),
                    deps=[drain_done[(s, t)]])
                out_dma[(s, t)] = tok
                tok_y_final.append(tok)

        last_join = P.join()
        for g in reversed(otg):
            g.__exit__(None, None, None)

    if DEBUG:
        stdump, _ = sballoc("stdump", [128, 8], F32)
        for ii, (nm2, pp) in enumerate((("mean", 0), ("mean", 1), ("rs", 0), ("rs", 1),
                                        ("var", 0), ("var", 1), ("sm", 0), ("sm", 1))):
            P.emit("dve", lambda e, ii=ii, nm2=nm2, pp=pp: e.tensor_copy(
                stdump[:, ii:ii + 1], stat[nm2][pp][:]), deps=last_join)
        P.emit("gp", lambda e: e.dma_start(out=d_dbg["dst"][0], in_=stdump[:]), deps=P.join())
        for nm, src in (("dkT", kT), ("dvN", vN), ("dxnT", xnT), ("datT", atT),
                        ("dmidp", midp), ("dgbf", gbf)):
            P.emit("gp", lambda e, nm=nm, src=src: e.dma_start(out=d_dbg[nm][0], in_=src[:]),
                   deps=last_join)
        P.final_waits.append(("act", "gpd", 16 * P.dma_cnt["gp"]))
    P.final_waits.append(("gp", "syd", 16 * P.dma_cnt["sync"]))
    P.final_waits.append(("act", "syd", 16 * P.dma_cnt["sync"]))

    with (
        nc.semaphore("s_pe") as s_pe,
        nc.semaphore("s_act") as s_act,
        nc.semaphore("s_dve") as s_dve,
        nc.semaphore("s_syd") as s_syd,
        nc.semaphore("s_gpd") as s_gpd,
        nc.Block() as block,
    ):
        sems = {"pe": s_pe, "act": s_act, "dve": s_dve, "syd": s_syd, "gpd": s_gpd}

        def replay(eng_obj, name):
            for waits, fn, inc in P.ops[name]:
                for (s, v) in waits:
                    eng_obj.wait_ge(sems[s], v)
                fn(eng_obj).then_inc(sems[inc[0]], inc[1])
            for (e2, s, v) in P.final_waits:
                if e2 == name:
                    eng_obj.wait_ge(sems[s], v)

        @block.tensor
        def _(e):
            replay(e, "pe")

        @block.scalar
        def _(e):
            replay(e, "act")

        @block.vector
        def _(e):
            replay(e, "dve")

        @block.sync
        def _(e):
            replay(e, "sync")

        @block.gpsimd
        def _(e):
            replay(e, "gp")

    return nc


_NC_CACHE = [None]


def kernel(L0_attn_x, L0_attn_Wq, L0_attn_Wk, L0_attn_Wv, L0_attn_Wo,
           L0_ffn_A, L0_ffn_B, L0_ffn_Wr):
    x = np.ascontiguousarray(np.asarray(L0_attn_x).reshape(S_TOT, DM), dtype=np.float32)
    Wq = np.asarray(L0_attn_Wq, dtype=np.float32)
    Wk = np.asarray(L0_attn_Wk, dtype=np.float32)
    Wv = np.asarray(L0_attn_Wv, dtype=np.float32)
    Wo = np.asarray(L0_attn_Wo, dtype=np.float32)
    A = np.asarray(L0_ffn_A, dtype=np.float32)
    B = np.asarray(L0_ffn_B, dtype=np.float32)
    Wr = np.asarray(L0_ffn_Wr, dtype=np.float32)

    wq_arr = np.ascontiguousarray(Wq.reshape(H, 128, 32, 128).transpose(0, 3, 2, 1))
    wo_arr = np.ascontiguousarray(Wo.reshape(8, 512, 8, 4, 128).transpose(0, 2, 4, 3, 1))
    wk_arr = np.ascontiguousarray(Wk.reshape(128, 32, 128).transpose(2, 1, 0))
    wv_arr = np.ascontiguousarray(Wv.reshape(128, 32, 128).transpose(2, 1, 0))
    awr_arr = np.ascontiguousarray(A.reshape(64, 32, 128).transpose(2, 1, 0))
    wr_arr = np.ascontiguousarray(Wr.reshape(4, 32, 128).transpose(2, 1, 0))
    bt_arr = np.ascontiguousarray(B.transpose(0, 2, 1).reshape(64, 8, 512).transpose(1, 0, 2))
    idb = np.eye(128, dtype=np.float32).astype(BF)
    idf = np.eye(128, dtype=np.float32)
    exe = np.zeros((4, 64), dtype=np.float32)
    for e in range(4):
        exe[e, 16 * e:16 * e + 16] = 1.0
    exe = exe.astype(BF)
    ramp = np.linspace(0.0, 1.0, 256, dtype=np.float32)

    in_maps = []
    for c in range(NCORES):
        g0 = T * c
        lo, hi = g0 - 256, g0 + T + 256
        xh = np.zeros((HALO, DM), dtype=np.float32)
        a0, a1 = max(lo, 0), min(hi, S_TOT)
        xh[a0 - lo:a1 - lo] = x[a0:a1]
        xr = np.ascontiguousarray(
            x[g0:g0 + T].reshape(2, 8, 128, 8, 512).transpose(0, 3, 2, 1, 4))
        pos = np.arange(lo, hi)
        cosc, sinc = rope_cache(pos)
        cosT = np.ascontiguousarray(cosc.T * SC_HALF).astype(BF)
        sinneg = sinc.T * SC_HALF
        sinneg[0:64, :] = -sinneg[0:64, :]
        sinT = np.ascontiguousarray(sinneg).astype(BF)
        alf = np.zeros((128, 16), dtype=np.float32)
        omaa = np.zeros((128, 16), dtype=np.float32)
        corr = np.zeros((128, 16), dtype=np.float32)
        for sg in range(8):
            for qt in range(2):
                col = 2 * sg + qt
                av = ramp[128 * qt:128 * qt + 128]
                if c == 0 and sg == 0:
                    alf[:, col] = 1.0
                    omaa[:, col] = 0.0
                else:
                    alf[:, col] = av
                    omaa[:, col] = 1.0 - av
                if c == NCORES - 1 and sg == 7:
                    corr[:, col] = 128.0
        in_maps.append({
            "xh": xh, "xr": xr, "wq": wq_arr, "wo": wo_arr, "wk": wk_arr,
            "wv": wv_arr, "awr": awr_arr, "wr": wr_arr, "bt": bt_arr,
            "cosT": cosT, "sinT": sinT, "alf": alf, "oma": omaa, "cor": corr,
            "idb": idb, "idf": idf, "exe": exe,
        })

    if _NC_CACHE[0] is None:
        _NC_CACHE[0] = build_program()
    res = run_bass_kernel_spmd(_NC_CACHE[0], in_maps, core_ids=list(range(NCORES)))
    out = np.concatenate([r["y"] for r in res.results], axis=0)
    return out.reshape(1, S_TOT, DM).astype(np.float32)
